# version 01 (final)
# baseline (speedup 1.0000x reference)
import numpy as np
import ml_dtypes
import concourse.bass as bass
import concourse.mybir as mybir
from concourse.bass import IndirectOffsetOnAxis
from concourse.bass_utils import run_bass_kernel_spmd

F32 = mybir.dt.float32
BF16 = mybir.dt.bfloat16
I32 = mybir.dt.int32
AF = mybir.ActivationFunctionType
OP = mybir.AluOpType
AX = mybir.AxisListType.X

D = 1024
DI = 2048
DS = 16
DTR = 64
TLOC = 2048
HALO = 3
TW = 2051
W = 512
TCW = 1024
CH_HALF = 1024
NHC = 8
SND_R = CH_HALF + 128
AG_R = 2 * SND_R
NEXP = 8
DFF = 1024
LN_EPS = 1e-5
RMS_EPS = 1e-6

GROUPS = [[0, 1], [2, 3], [4, 5], [6, 7]]
ENGINES = ("sync", "gpsimd", "scalar", "vector", "tensor")


def host_prepare(inp):
    f = np.ascontiguousarray
    x = np.asarray(inp["L0_attn_x"], np.float32)

    def chvec(v, n):
        return f(np.asarray(v, np.float32).reshape(n, 128).T)

    shared = {
        "mg": chvec(inp["L0_attn_norm_g"], 8),
        "mb": chvec(inp["L0_attn_norm_b"], 8),
        "cb": chvec(inp["L0_attn_conv_b"], 16),
        "cw": f(np.asarray(inp["L0_attn_conv_w"], np.float32).reshape(16, 128, 4)
                .transpose(1, 0, 2).reshape(128, 64)),
        "dtb": chvec(inp["L0_attn_dt_proj_b"], 16),
        "g1v": chvec(inp["L0_ffn_g1"], 8),
        "g2v": chvec(inp["L0_ffn_g2"], 8),
        "ebias": f(np.broadcast_to(np.asarray(inp["L0_ffn_expert_bias"], np.float32)[None, :],
                                   (128, 8)).copy()),
        "wip_t": f(np.asarray(inp["L0_attn_in_proj_w"], np.float32).T),
        "wxp_t": f(np.asarray(inp["L0_attn_x_proj_w"], np.float32).T),
        "wdt_t": f(np.asarray(inp["L0_attn_dt_proj_w"], np.float32).T),
        "wq_t": f(np.asarray(inp["L0_ffn_Wq"], np.float32).T),
        "wkc_t": f(np.asarray(inp["L0_ffn_Wkc"], np.float32).T),
        "wvc_t": f(np.asarray(inp["L0_ffn_Wvc"], np.float32).T),
        "wo_t": f(np.asarray(inp["L0_ffn_Wo"], np.float32).T),
        "wr_t": f(np.asarray(inp["L0_ffn_Wr"], np.float32).T),
        "sw1_t": f(np.asarray(inp["L0_ffn_sW1"], np.float32).T),
        "sw2_t": f(np.asarray(inp["L0_ffn_sW2"], np.float32).T),
        "sw3_t": f(np.asarray(inp["L0_ffn_sW3"], np.float32).T),
        "ew1_t": f(np.asarray(inp["L0_ffn_eW1"], np.float32).transpose(0, 2, 1)),
        "ew2_t": f(np.asarray(inp["L0_ffn_eW2"], np.float32).transpose(0, 2, 1)),
        "ew3_t": f(np.asarray(inp["L0_ffn_eW3"], np.float32).transpose(0, 2, 1)),
        "idn": f(np.eye(128, dtype=ml_dtypes.bfloat16)),
        "idn32": f(np.eye(128, dtype=np.float32)),
    }
    A_full = np.asarray(inp["L0_attn_log_A"], np.float32)
    Dp_full = np.asarray(inp["L0_attn_Dp"], np.float32)
    wout = np.asarray(inp["L0_attn_out_proj_w"], np.float32)

    in_maps = []
    for c in range(8):
        b, h = c // 2, c % 2
        t0 = h * TLOC
        xw = np.zeros((TW, D), np.float32)
        lo = max(t0 - HALO, 0)
        xw[HALO - (t0 - lo):] = x[b, lo:t0 + TLOC]
        x_fm = f(xw.T)
        my = slice(h * CH_HALF, (h + 1) * CH_HALF)
        A_my = -np.exp(A_full[my])
        A_sc = f(A_my.reshape(8, 128, 16).transpose(1, 0, 2).reshape(128, 128))
        Dp_sc = chvec(Dp_full[my], 8)
        wout_t_half = f(wout.T[my])

        pr = np.arange(128, dtype=np.int32)
        cols = []
        for tc in range(4):
            for dch in range(NHC):
                cols.append(tc * DI + h * CH_HALF + dch * 128 + pr)
        idx_all = f(np.stack(cols, 1).astype(np.int32))

        m = {"x_fm": x_fm, "A_sc": A_sc, "Dp_sc": Dp_sc,
             "wout_t_half": wout_t_half, "idx_all": idx_all}
        m.update(shared)
        in_maps.append(m)
    return in_maps


def host_finalize(results):
    out = np.empty((4, 4096, 1024), np.float32)
    for c in range(8):
        b, h = c // 2, c % 2
        out[b, h * TLOC:(h + 1) * TLOC] = np.asarray(results[c]["out_fm"]).T
    return out


class Sem:
    def __init__(self, nc, name):
        self.h = nc.semaphore(name).__enter__()
        self.n = 0


class P:
    def __init__(self, nc):
        self.nc = nc
        self.streams = {e: [] for e in ENGINES}
        for s in ("ld", "gd", "mm", "ve", "sc", "cc", "st"):
            setattr(self, s, Sem(nc, "s_" + s))

    def barrier(self):
        for eng in ENGINES:
            for nm in ("ld", "gd", "mm", "ve", "sc", "cc"):
                self.wait(eng, getattr(self, nm))

    def wait(self, eng, sem, val=None):
        v = sem.n if val is None else val
        if v <= 0:
            return
        self.streams[eng].append(lambda e, s=sem.h, vv=v: e.wait_ge(s, vv))

    def emit(self, eng, meth, *args, inc=None, **kwargs):
        if inc is not None:
            sem, k = inc
            sem.n += k

            def fn(e, a=args, kw=kwargs, s=sem.h, kk=k, m=meth):
                getattr(e, m)(*a, **kw).then_inc(s, kk)
        else:
            def fn(e, a=args, kw=kwargs, m=meth):
                getattr(e, m)(*a, **kw)
        self.streams[eng].append(fn)

    def dma(self, out, in_, eng="sync", sem=None, **kw):
        if sem is None:
            sem = self.gd if eng == "gpsimd" else self.ld
        self.emit(eng, "dma_start", inc=(sem, 16), out=out, in_=in_, **kw)

    def scat(self, out, idx, in_):
        self.emit("gpsimd", "indirect_dma_start", inc=(self.gd, 16),
                  out=out, out_offset=IndirectOffsetOnAxis(ap=idx, axis=0),
                  in_=in_, in_offset=None)

    def fence(self, g, ones_src):
        self.emit("gpsimd", "dma_start", inc=(self.gd, 16),
                  out=g["fence_scr"][0:1, 0:1], in_=ones_src)

    def mmul(self, out, lhsT, rhs, first, last, inc_last=None, inc_force=False):
        if inc_last is None:
            inc_last = last
        self.emit("tensor", "matmul", out, lhsT, rhs, start=first, stop=last,
                  inc=(self.mm, 1) if (inc_last or inc_force) else None)

    def vop(self, meth, *args, inc=True, **kw):
        self.emit("vector", meth, *args, inc=(self.ve, 1) if inc else None, **kw)

    def sop(self, meth, *args, inc=True, **kw):
        self.emit("scalar", meth, *args, inc=(self.sc, 1) if inc else None, **kw)


class Alloc:
    def __init__(self, nc):
        self.nc = nc
        self.stack = []

    def sb(self, name, shape, dtype):
        cm = self.nc.sbuf_tensor(name, shape, dtype)
        t = cm.__enter__()
        self.stack.append((name, cm))
        return t

    def free(self, *names):
        want = set(names)
        top = [n for n, _ in self.stack[-len(want):]]
        assert set(top) == want, f"non-stack free: want {want}, top {top}"
        for _ in range(len(want)):
            n, cm = self.stack.pop()
            cm.__exit__(None, None, None)


def build_nc(taps=(), phases="full"):
    nc = bass.Bass("TRN2", target_bir_lowering=False, debug=False)
    g = {}
    al = Alloc(nc)

    def din(name, shape, dtype=F32):
        g[name] = nc.dram_tensor(name, shape, dtype, kind="ExternalInput").ap()

    def dout(name, shape, dtype=F32):
        g[name] = nc.dram_tensor(name, shape, dtype, kind="ExternalOutput").ap()

    def dtmp(name, shape, dtype=F32):
        g[name] = nc.dram_tensor(name, shape, dtype).ap()

    din("x_fm", [D, TW]); din("A_sc", [128, 128]); din("Dp_sc", [128, 8])
    din("wout_t_half", [CH_HALF, D]); din("idx_all", [128, 32], I32)
    for nm, wd_ in (("mg", 8), ("mb", 8), ("cb", 16), ("cw", 64), ("dtb", 16),
                    ("g1v", 8), ("g2v", 8), ("ebias", 8)):
        din(nm, [128, wd_])
    din("wip_t", [D, 2 * DI]); din("wxp_t", [DI, 96]); din("wdt_t", [DTR, DI])
    din("wq_t", [D, D]); din("wkc_t", [D, 256]); din("wvc_t", [D, 256])
    din("wo_t", [D, D]); din("wr_t", [D, 8])
    din("sw1_t", [D, DFF]); din("sw2_t", [DFF, D]); din("sw3_t", [D, DFF])
    din("ew1_t", [NEXP, D, DFF]); din("ew2_t", [NEXP, DFF, D]); din("ew3_t", [NEXP, D, DFF])
    din("idn", [128, 128], BF16); din("idn32", [128, 128])
    dout("out_fm", [D, TLOC])

    for nm in ("snd_d", "snd_xs", "snd_z"):
        dtmp(nm, [2 * DI, TCW], BF16)
    for nm in ("ag_d", "ag_xs", "ag_z"):
        dtmp(nm, [4 * DI, TCW], BF16)
    dtmp("bc_snd", [64, TCW], BF16)
    dtmp("ag_bc", [128, TCW], BF16)
    dtmp("rs_in_a", [2 * D, TCW]); dtmp("rs_in_b", [2 * D, TCW])
    dtmp("rs_out_a", [D, TCW]); dtmp("rs_out_b", [D, TCW])
    dtmp("xmu_sp", [D, TLOC])
    dtmp("x2_sp", [D, TLOC]); dtmp("xf_sp", [D, TLOC]); dtmp("x3_sp", [D, TLOC])
    dtmp("kc_snd", [256, TLOC], BF16); dtmp("ag_kc", [512, TLOC], BF16)
    dtmp("vct_snd", [TLOC, 256], BF16); dtmp("ag_vc", [2 * TLOC, 256], BF16)
    dtmp("rowb", [1, 4096])
    dtmp("fence_scr", [1, 64])
    dtmp("rowbf", [1, 4096], BF16)
    dtmp("comb_fm", [16, TLOC])

    tap_specs = {
        "t_xs": ([DI, TLOC], BF16), "t_zs": ([DI, TLOC], BF16),
        "t_delta": ([DI, TLOC], BF16), "t_bc": ([32, TLOC], BF16),
        "t_y": ([CH_HALF, 2 * TLOC], F32), "t_x2": ([D, TLOC], F32),
        "t_nx1": ([D, TLOC], BF16), "t_ao": ([D, TLOC], F32),
        "t_x3": ([D, TLOC], F32), "t_comb": ([16, TLOC], F32),
        "t_xmu": ([D, TLOC], F32), "t_r1": ([1, TLOC], F32),
        "t_xn": ([D, TW], BF16),
        "t_attf": ([128, 4 * W], F32),
        "t_vc": ([256, 256], BF16),
        "t_attb": ([128, 2 * W], BF16),
        "t_rows": ([1, 6 * W], F32),
        "t_xsp": ([128, TW], BF16),
        "t_cv": ([128, W], F32),
        "t_cvw": ([128, W + HALO], F32),
        "t_xg": ([128, W], BF16),
        "t_rhot": ([128, 16], F32),
    }
    for t in taps:
        dout(t, *tap_specs[t])

    p = P(nc)

    idx_sb = al.sb("idx_sb", [128, 32], I32)
    p.dma(idx_sb[:, :], g["idx_all"][:, :])
    IDX_SCAN = lambda tc, dch: idx_sb[:, tc * NHC + dch:tc * NHC + dch + 1]

    ones32 = al.sb("ones32", [128, 1], F32)
    onesbf = al.sb("onesbf", [128, 128], BF16)
    onesr = al.sb("onesr", [1, 128], F32)
    p.vop("memset", ones32[:, :], 1.0, inc=False)
    p.vop("memset", onesbf[:, :], 1.0, inc=False)
    p.vop("memset", onesr[:, :], 1.0)
    vec = {}
    for nm, wd_ in (("mg", 8), ("mb", 8), ("cb", 16), ("cw", 64), ("dtb", 16),
                    ("g1v", 8), ("g2v", 8), ("ebias", 8), ("A_sc", 128), ("Dp_sc", 8)):
        vec[nm] = al.sb("v_" + nm, [128, wd_], F32)
        p.dma(vec[nm][:, :], g[nm][:, :])
    idn_sb = al.sb("idn_sb", [128, 128], BF16)
    p.dma(idn_sb[:, :], g["idn"][:, :])
    idn32_sb = al.sb("idn32_sb", [128, 128], F32)
    p.dma(idn32_sb[:, :], g["idn32"][:, :])

    PPN = 8
    pp = []
    for i in range(PPN):
        cm = nc.psum_tensor(f"pp{i}", [128, 512], F32)
        pp.append(cm.__enter__())
    ppg = [[] for _ in range(PPN)]

    def guard(i):
        for s_, v_ in ppg[i]:
            p.wait("tensor", s_, v_)
        ppg[i] = []

    x_sb = al.sb("x_sb", [128, 8 * W], F32)
    sq_sb = al.sb("sq_sb", [128, 2 * W], F32)
    row_sb = al.sb("row_sb", [1, 6 * W], F32)
    mu_rep = al.sb("mu_rep", [128, W], F32)
    rho_rep = al.sb("rho_rep", [128, W], F32)
    t32a = al.sb("t32a", [128, W], F32)
    t32b = al.sb("t32b", [128, W], F32)
    x2_sb = al.sb("x2_sb", [128, 8 * W], F32)

    def ln_stats(src_sb, wd, eps, rms_only=False):
        guard(6); guard(7)
        p.wait("tensor", p.ve); p.wait("tensor", p.ld)
        if not rms_only:
            for ch in range(8):
                p.mmul(pp[6][0:1, 0:wd], ones32[:, :], src_sb[:, ch * W:ch * W + wd],
                       ch == 0, ch == 7)
        p.wait("scalar", p.ld); p.wait("scalar", p.ve)
        mm_at = []
        for ch in range(8):
            sq = sq_sb[:, (ch % 2) * W:(ch % 2) * W + wd]
            if ch >= 2:
                p.wait("scalar", p.mm, mm_at[ch - 2])
            p.sop("activation", sq, src_sb[:, ch * W:ch * W + wd], AF.Square)
            p.wait("tensor", p.sc)
            p.mmul(pp[7][0:1, 0:wd], ones32[:, :], sq, ch == 0, ch == 7, inc_force=True)
            mm_at.append(p.mm.n)
        p.wait("vector", p.mm)
        p.wait("vector", p.sc)
        R = row_sb
        mu, m2, var, t1, r1, srt = (R[:, i * W:i * W + wd] for i in range(6))
        if rms_only:
            p.vop("tensor_scalar", t1, pp[7][0:1, 0:wd], 1.0 / D, eps, OP.mult, OP.add,
                  inc=False)
            p.vop("reciprocal", r1, t1)
            ppg[7].append((p.ve, p.ve.n))
        else:
            p.vop("tensor_scalar_mul", mu, pp[6][0:1, 0:wd], 1.0 / D, inc=False)
            p.vop("tensor_mul", m2, mu, mu, inc=False)
            p.vop("scalar_tensor_tensor", var, pp[7][0:1, 0:wd], 1.0 / D, m2,
                  OP.mult, OP.subtract, inc=False)
            p.vop("tensor_scalar_add", t1, var, eps, inc=False)
            p.vop("reciprocal", r1, t1)
            ppg[6].append((p.ve, p.ve.n)); ppg[7].append((p.ve, p.ve.n))
        p.wait("scalar", p.ve)
        p.sop("activation", srt, r1, AF.Sqrt)
        guard(4)
        p.wait("tensor", p.sc)
        p.mmul(pp[4][:, 0:wd], onesr[:, :], srt, True, True)
        if not rms_only:
            guard(5)
            p.mmul(pp[5][:, 0:wd], onesr[:, :], mu, True, True)
        p.wait("vector", p.mm)
        p.vop("tensor_copy", rho_rep[:, 0:wd], pp[4][:, 0:wd], inc=not rms_only)
        ppg[4].append((p.ve, p.ve.n))
        if not rms_only:
            p.vop("tensor_copy", mu_rep[:, 0:wd], pp[5][:, 0:wd])
            ppg[5].append((p.ve, p.ve.n))

    r1_row = al.sb("r1_row", [1, TW], F32)
    xn_sb = al.sb("xn_sb", [128, 8 * TW], BF16)

    tiles = [(k, W * k, W) for k in range(4)] + [(4, TW - W, W)]
    for tk, c0, wd in tiles:
        p.wait("sync", p.ve)
        p.wait("sync", p.sc)
        p.wait("sync", p.mm)
        for ch in range(8):
            p.dma(x_sb[:, ch * W:ch * W + wd],
                  g["x_fm"][128 * ch:128 * (ch + 1), c0:c0 + wd])
        ln_stats(x_sb, wd, LN_EPS)
        R = row_sb
        mu, m2, var, t1, r1, srt = (R[:, i * W:i * W + wd] for i in range(6))
        p.vop("tensor_mul", m2, var, r1, inc=False)
        p.vop("tensor_scalar_add", m2, m2, LN_EPS, inc=False)
        p.vop("reciprocal", m2, m2, inc=False)
        p.vop("tensor_mul", m2, m2, r1)
        p.wait("scalar", p.ve)
        p.sop("activation", m2, m2, AF.Sqrt)
        p.sop("activation", r1_row[:, c0:c0 + wd], r1, AF.Sqrt)
        guard(5)
        p.wait("tensor", p.sc)
        p.mmul(pp[5][:, 0:wd], onesr[:, :], m2, True, True)
        p.wait("vector", p.mm)
        p.vop("tensor_copy", rho_rep[:, 0:wd], pp[5][:, 0:wd])
        ppg[5].append((p.ve, p.ve.n))
        lo = max(c0, HALO)
        for ch in range(8):
            p.vop("tensor_sub", t32a[:, 0:wd], x_sb[:, ch * W:ch * W + wd],
                  mu_rep[:, 0:wd], inc=False)
            if c0 + wd > HALO:
                p.wait("vector", p.ld)
                p.vop("tensor_copy", t32b[:, 0:wd], t32a[:, 0:wd])
                p.wait("sync", p.ve)
                p.dma(g["xmu_sp"][128 * ch:128 * (ch + 1), lo - HALO:c0 + wd - HALO],
                      t32b[:, lo - c0:wd])
            p.vop("tensor_mul", t32a[:, 0:wd], t32a[:, 0:wd], rho_rep[:, 0:wd], inc=False)
            p.vop("tensor_scalar", xn_sb[:, ch * TW + c0:ch * TW + c0 + wd],
                  t32a[:, 0:wd], vec["mg"][:, ch:ch + 1], vec["mb"][:, ch:ch + 1],
                  OP.mult, OP.add, inc=True)
    if "t_xn" in g:
        p.wait("sync", p.ve)
        for ch in range(8):
            p.dma(g["t_xn"][128 * ch:(ch + 1) * 128, :], xn_sb[:, ch * TW:(ch + 1) * TW])
    if "t_rows" in g:
        p.wait("sync", p.ve); p.wait("sync", p.sc)
        p.dma(g["t_rows"][:, :], row_sb[0:1, :])
    if "t_rhot" in g:
        p.wait("sync", p.ve)
        p.dma(g["t_rhot"][:, :], rho_rep[:, 0:16])
    if "t_xmu" in g:
        p.wait("sync", p.ve)
        p.dma(g["t_xmu"][:, :], g["xmu_sp"][:, :])
    if "t_r1" in g:
        p.wait("sync", p.sc)
        p.dma(g["t_r1"][:, :], r1_row[0:1, HALO:TW])

    wip_sb = al.sb("wip_sb", [128, 8 * 2 * DI], BF16)
    for c8 in range(8):
        p.dma(wip_sb[:, c8 * 2 * DI:(c8 + 1) * 2 * DI],
              g["wip_t"][128 * c8:128 * (c8 + 1), :], eng="gpsimd")
    wxp_sb = al.sb("wxp_sb", [128, 16 * 96], BF16)
    for c16 in range(16):
        p.dma(wxp_sb[:, c16 * 96:(c16 + 1) * 96],
              g["wxp_t"][128 * c16:128 * (c16 + 1), :], eng="gpsimd")
    wdt_sb = al.sb("wdt_sb", [DTR, DI], BF16)
    p.dma(wdt_sb[:, :], g["wdt_t"][:, :], eng="gpsimd")
    gd_weights = p.gd.n

    xsp_g = al.sb("xsp_g", [128, TW], BF16)
    xs_g = al.sb("xs_g", [128, TLOC], BF16)
    cvw = al.sb("cvw", [128, W + HALO], F32)
    cv_a = al.sb("cv_a", [128, W], F32)
    cv_b = al.sb("cv_b", [128, W], F32)
    dtbc = al.sb("dtbc", [96, TLOC], BF16)

    p.wait("tensor", p.gd, gd_weights)
    p.wait("tensor", p.ve)
    for gch in range(16):
        for tk, c0, wd in tiles:
            q = tk % 2
            guard(q)
            for ch in range(8):
                p.mmul(pp[q][:, 0:wd],
                       wip_sb[:, ch * 2 * DI + gch * 128:ch * 2 * DI + gch * 128 + 128],
                       xn_sb[:, ch * TW + c0:ch * TW + c0 + wd], ch == 0, ch == 7)
            p.wait("vector", p.mm)
            p.wait("vector", p.gd)
            p.vop("tensor_copy", xsp_g[:, c0:c0 + wd], pp[q][:, 0:wd])
            ppg[q].append((p.ve, p.ve.n))
        if gch == 0 and "t_xsp" in g:
            p.wait("sync", p.ve)
            p.dma(g["t_xsp"][:, :], xsp_g[:, :])
        for kt in range(4):
            t0 = kt * W
            p.wait("vector", p.ve)
            p.wait("vector", p.sc)
            p.vop("tensor_copy", cvw[:, :], xsp_g[:, t0:t0 + W + HALO], inc=False)
            p.vop("tensor_scalar", cv_a[:, :], cvw[:, 0:W],
                  vec["cw"][:, gch * 4 + 0:gch * 4 + 1], vec["cb"][:, gch:gch + 1],
                  OP.mult, OP.add, inc=False)
            for j in (1, 2, 3):
                src, dst = (cv_a, cv_b) if j % 2 == 1 else (cv_b, cv_a)
                p.vop("scalar_tensor_tensor", dst[:, :], cvw[:, j:W + j],
                      vec["cw"][:, gch * 4 + j:gch * 4 + j + 1], src[:, :],
                      OP.mult, OP.add, inc=(j == 3))
            if gch == 0 and kt == 0 and "t_cv" in g:
                p.wait("sync", p.ve)
                p.dma(g["t_cv"][:, :], cv_b[:, :])
                p.dma(g["t_cvw"][:, :], cvw[:, :])
            p.wait("scalar", p.ve)
            p.wait("scalar", p.gd)
            p.wait("scalar", p.mm)
            p.wait("scalar", p.ld)
            p.sop("activation", xs_g[:, t0:t0 + W], cv_b[:, :], AF.Silu)
            if gch == 0 and kt == 0 and "t_xg" in g:
                p.wait("sync", p.sc)
                p.dma(g["t_xg"][:, :], xs_g[:, 0:W])
            qx = 2 + kt
            if gch == 0:
                guard(qx)
            p.wait("tensor", p.sc)
            p.mmul(pp[qx][0:96, 0:W], wxp_sb[:, gch * 96:(gch + 1) * 96],
                   xs_g[:, t0:t0 + W], gch == 0, gch == 15, inc_force=True)
        p.wait("sync", p.sc)
        for lt in range(2):
            p.dma(g["snd_xs"][lt * DI + gch * 128:lt * DI + (gch + 1) * 128, :],
                  xs_g[:, lt * TCW:(lt + 1) * TCW])
    p.wait("scalar", p.mm)
    for kt in range(4):
        p.sop("activation", dtbc[:, kt * W:(kt + 1) * W], pp[2 + kt][0:96, 0:W], AF.Copy)
        ppg[2 + kt].append((p.sc, p.sc.n))
    p.wait("sync", p.sc)
    for lt in range(2):
        p.dma(g["bc_snd"][lt * 32:(lt + 1) * 32, :], dtbc[64:96, lt * TCW:(lt + 1) * TCW])
    if "t_bc" in g:
        p.wait("sync", p.ld)
        for lt in range(2):
            p.dma(g["t_bc"][:, lt * TCW:(lt + 1) * TCW], g["bc_snd"][lt * 32:(lt + 1) * 32, :])

    zbf = al.sb("zbf", [128, TLOC], BF16)
    for gch in range(16):
        eb = 16 + gch
        p.wait("scalar", p.ld)
        for tk, c0, wd in tiles:
            if c0 + wd <= HALO:
                continue
            q = tk % 2
            guard(q)
            for ch in range(8):
                p.mmul(pp[q][:, 0:wd],
                       wip_sb[:, ch * 2 * DI + eb * 128:ch * 2 * DI + eb * 128 + 128],
                       xn_sb[:, ch * TW + c0:ch * TW + c0 + wd], ch == 0, ch == 7)
            p.wait("scalar", p.mm)
            zlo = max(c0, HALO)
            p.sop("activation", zbf[:, zlo - HALO:c0 + wd - HALO],
                  pp[q][:, zlo - c0:wd], AF.Silu)
            ppg[q].append((p.sc, p.sc.n))
        p.wait("sync", p.sc)
        for lt in range(2):
            p.dma(g["snd_z"][lt * DI + gch * 128:lt * DI + (gch + 1) * 128, :],
                  zbf[:, lt * TCW:(lt + 1) * TCW])
    if "t_zs" in g:
        p.wait("sync", p.ld)
        for lt in range(2):
            p.dma(g["t_zs"][:, lt * TCW:(lt + 1) * TCW], g["snd_z"][lt * DI:(lt + 1) * DI, :])
    if "t_xs" in g:
        p.wait("sync", p.ld)
        for lt in range(2):
            p.dma(g["t_xs"][:, lt * TCW:(lt + 1) * TCW], g["snd_xs"][lt * DI:(lt + 1) * DI, :])

    sp_t1 = al.sb("sp_t1", [128, W], F32)
    sp_t2 = al.sb("sp_t2", [128, W], BF16)
    dl_bf = al.sb("dl_bf", [128, TLOC], BF16)
    p.wait("tensor", p.sc)
    for dch in range(16):
        p.wait("vector", p.ld)
        for kt in range(4):
            t0 = kt * W
            q = 6
            guard(q)
            p.mmul(pp[q][:, 0:W], wdt_sb[:, dch * 128:(dch + 1) * 128],
                   dtbc[0:64, t0:t0 + W], True, True)
            p.wait("scalar", p.mm)
            p.wait("scalar", p.ve)
            p.sop("activation", sp_t1[:, :], pp[q][:, 0:W], AF.Abs,
                  bias=vec["dtb"][:, dch:dch + 1])
            p.sop("activation", sp_t1[:, :], sp_t1[:, :], AF.Exp, scale=-1.0)
            p.sop("activation", sp_t1[:, :], sp_t1[:, :], AF.Ln, bias=1.0)
            p.sop("activation", sp_t2[:, :], pp[q][:, 0:W], AF.Relu,
                  bias=vec["dtb"][:, dch:dch + 1])
            ppg[q].append((p.sc, p.sc.n))
            p.wait("vector", p.sc)
            p.vop("tensor_add", dl_bf[:, t0:t0 + W], sp_t2[:, :], sp_t1[:, :])
        p.wait("sync", p.ve)
        for lt in range(2):
            p.dma(g["snd_d"][lt * DI + dch * 128:lt * DI + (dch + 1) * 128, :],
                  dl_bf[:, lt * TCW:(lt + 1) * TCW])
    if "t_delta" in g:
        p.wait("sync", p.ld)
        for lt in range(2):
            p.dma(g["t_delta"][:, lt * TCW:(lt + 1) * TCW], g["snd_d"][lt * DI:(lt + 1) * DI, :])

    p.barrier()
    al.free("wip_sb", "wxp_sb", "wdt_sb", "xn_sb", "xsp_g", "xs_g", "cvw",
            "cv_a", "cv_b", "zbf", "sp_t1", "sp_t2", "dl_bf", "dtbc")

    p.wait("gpsimd", p.gd)
    p.wait("gpsimd", p.ld)
    for nm_i, nm_o in (("snd_d", "ag_d"), ("snd_xs", "ag_xs"),
                       ("snd_z", "ag_z"), ("bc_snd", "ag_bc")):
        p.emit("gpsimd", "collective_compute", "AllGather", OP.bypass,
               replica_groups=GROUPS, ins=[g[nm_i][:, :]], outs=[g[nm_o][:, :]],
               inc=(p.cc, 1))
    own_done = p.gd.n

    wo_sb = al.sb("wo_sb", [128, 8 * D], BF16)
    for ch in range(8):
        p.dma(wo_sb[:, ch * D:(ch + 1) * D],
              g["wout_t_half"][128 * ch:128 * (ch + 1), :], eng="gpsimd")
    gd_wo = p.gd.n

    dl_sb = al.sb("dl_sb", [128, 8 * TCW], BF16)
    xs_sb = al.sb("xs_sb", [128, 8 * TCW], BF16)
    dx_sb = al.sb("dx_sb", [128, 8 * TCW], BF16)
    zrow = al.sb("zrow", [128, TCW], BF16)
    brep = al.sb("brep", [128, TCW], BF16)
    crep = al.sb("crep", [128, TCW], BF16)
    dA_sb = al.sb("dA_sb", [128, TCW], F32)
    u_sb = al.sb("u_sb", [128, TCW], BF16)
    h_sb = al.sb("h_sb", [128, TCW], BF16)
    gt_sb = al.sb("gt_sb", [128, TCW], BF16)
    y_acc = al.sb("y_acc", [128, 8 * TCW], F32)
    yg_sb = al.sb("yg_sb", [128, 8 * TCW], BF16)
    hst = al.sb("hst", [128, 8 * DS], F32)
    p.vop("memset", hst[:, :], 0.0)

    for tc in range(4):
        blk = tc // 2
        lt = (tc % 2) * TCW
        p.wait("gpsimd", p.cc)
        p.wait("gpsimd", p.ve)
        for dch in range(8):
            p.emit("gpsimd", "indirect_dma_start", inc=(p.gd, 16),
                   out=dl_sb[:, dch * TCW:(dch + 1) * TCW], out_offset=None,
                   in_=g["ag_d"][:, :],
                   in_offset=IndirectOffsetOnAxis(ap=IDX_SCAN(tc, dch), axis=0))
            p.emit("gpsimd", "indirect_dma_start", inc=(p.gd, 16),
                   out=xs_sb[:, dch * TCW:(dch + 1) * TCW], out_offset=None,
                   in_=g["ag_xs"][:, :],
                   in_offset=IndirectOffsetOnAxis(ap=IDX_SCAN(tc, dch), axis=0))
        ld_tiles = p.gd.n
        p.wait("vector", p.gd, ld_tiles)
        for dch in range(8):
            sl = slice(dch * TCW, (dch + 1) * TCW)
            p.vop("tensor_mul", dx_sb[:, sl], dl_sb[:, sl], xs_sb[:, sl], inc=False)
            p.vop("memset", y_acc[:, sl], 0.0, inc=(dch == 7))
        for s in range(DS):
            p.wait("gpsimd", p.ve)
            p.dma(brep[:, :], g["ag_bc"][tc * 32 + s:tc * 32 + s + 1, :]
                  .broadcast_to([128, TCW]), eng="gpsimd")
            p.dma(crep[:, :], g["ag_bc"][tc * 32 + 16 + s:tc * 32 + 17 + s, :]
                  .broadcast_to([128, TCW]), eng="gpsimd")
            bc_n = p.gd.n
            for dch in range(8):
                sl = slice(dch * TCW, (dch + 1) * TCW)
                p.wait("scalar", p.ve)
                p.sop("activation", dA_sb[:, :], dl_sb[:, sl], AF.Exp,
                      scale=vec["A_sc"][:, dch * 16 + s:dch * 16 + s + 1])
                p.wait("vector", p.gd, bc_n)
                p.wait("vector", p.sc)
                p.vop("tensor_mul", u_sb[:, :], dx_sb[:, sl], brep[:, :], inc=False)
                p.vop("tensor_tensor_scan", h_sb[:, :], dA_sb[:, :], u_sb[:, :],
                      hst[:, dch * DS + s:dch * DS + s + 1], OP.mult, OP.add, inc=False)
                p.vop("tensor_mul", gt_sb[:, :], h_sb[:, :], crep[:, :], inc=False)
                p.vop("tensor_add", y_acc[:, sl], y_acc[:, sl], gt_sb[:, :])
                p.wait("scalar", p.ve)
                p.sop("copy", hst[:, dch * DS + s:dch * DS + s + 1], h_sb[:, TCW - 1:TCW])
        p.wait("vector", p.mm)
        for dch in range(8):
            sl = slice(dch * TCW, (dch + 1) * TCW)
            p.wait("gpsimd", p.ve)
            p.emit("gpsimd", "indirect_dma_start", inc=(p.gd, 16),
                   out=zrow[:, :], out_offset=None, in_=g["ag_z"][:, :],
                   in_offset=IndirectOffsetOnAxis(ap=IDX_SCAN(tc, dch), axis=0))
            p.wait("vector", p.gd)
            p.vop("tensor_scalar_mul", gt_sb[:, :], xs_sb[:, sl],
                  vec["Dp_sc"][:, dch:dch + 1], inc=False)
            p.vop("tensor_add", y_acc[:, sl], y_acc[:, sl], gt_sb[:, :], inc=False)
            p.vop("tensor_mul", yg_sb[:, sl], zrow[:, :], y_acc[:, sl])
        if "t_y" in g:
            p.wait("sync", p.ve)
            for dch in range(8):
                p.dma(g["t_y"][dch * 128:(dch + 1) * 128, tc * TCW // 2:(tc + 1) * TCW // 2],
                      y_acc[:, dch * TCW:dch * TCW + TCW // 2])
        rsbuf = g["rs_in_a"] if tc % 2 == 0 else g["rs_in_b"]
        p.wait("tensor", p.ve)
        p.wait("tensor", p.gd, gd_wo)
        for dob in range(8):
            for sub in range(2):
                q = (2 * dob + sub) % 2
                guard(q)
                for dch in range(8):
                    p.mmul(pp[q][:, 0:W],
                           wo_sb[:, dch * D + dob * 128:dch * D + dob * 128 + 128],
                           yg_sb[:, dch * TCW + sub * W:dch * TCW + (sub + 1) * W],
                           dch == 0, dch == 7)
                p.wait("vector", p.mm)
                p.wait("vector", p.ld)
                sl_ev = slice(((2 * dob + sub) % 4) * W, ((2 * dob + sub) % 4) * W + W)
                p.vop("tensor_copy", x2_sb[:, sl_ev], pp[q][:, 0:W])
                ppg[q].append((p.ve, p.ve.n))
                p.wait("sync", p.ve)
                p.dma(rsbuf[blk * D + dob * 128:blk * D + (dob + 1) * 128,
                            sub * W:(sub + 1) * W], x2_sb[:, sl_ev])
        if tc == 2:
            p.wait("gpsimd", p.ld)
            p.emit("gpsimd", "collective_compute", "ReduceScatter", OP.add,
                   replica_groups=GROUPS, ins=[g["rs_in_a"][:, :]],
                   outs=[g["rs_out_a"][:, :]], inc=(p.cc, 1))
    p.wait("gpsimd", p.ld)
    p.emit("gpsimd", "collective_compute", "ReduceScatter", OP.add,
           replica_groups=GROUPS, ins=[g["rs_in_b"][:, :]],
           outs=[g["rs_out_b"][:, :]], inc=(p.cc, 1))
    rs_done = p.cc.n
    p.barrier()
    al.free("dl_sb", "xs_sb", "dx_sb", "zrow", "brep", "crep", "dA_sb", "u_sb",
            "h_sb", "gt_sb", "y_acc", "yg_sb", "hst", "wo_sb")

    p.wait("sync", p.sc)
    p.dma(g["rowb"][0:1, 0:TLOC], r1_row[0:1, HALO:TW])
    r1_ld = p.ld.n
    r1rep = al.sb("r1rep", [128, W], F32)
    xr_sb = al.sb("xr_sb", [128, 8 * W], F32)
    p.wait("sync", p.cc, rs_done)
    for kt in range(4):
        t0 = kt * W
        p.wait("gpsimd", p.ld, r1_ld)
        p.wait("gpsimd", p.ve)
        p.dma(r1rep[:, :], g["rowb"][0:1, t0:t0 + W].broadcast_to([128, W]), eng="gpsimd")
        rso = g["rs_out_a"] if (t0 // TCW) == 0 else g["rs_out_b"]
        ct = t0 % TCW
        p.wait("sync", p.ve)
        for ch in range(8):
            p.dma(x_sb[:, ch * W:(ch + 1) * W],
                  g["x_fm"][128 * ch:128 * (ch + 1), t0 + HALO:t0 + HALO + W])
            p.dma(xr_sb[:, ch * W:(ch + 1) * W],
                  g["xmu_sp"][128 * ch:128 * (ch + 1), t0:t0 + W])
        p.wait("vector", p.ld)
        p.wait("vector", p.gd)
        for ch in range(8):
            sl = slice(ch * W, (ch + 1) * W)
            p.vop("tensor_mul", x2_sb[:, sl], xr_sb[:, sl], r1rep[:, :], inc=False)
            p.vop("tensor_add", x2_sb[:, sl], x2_sb[:, sl], x_sb[:, sl], inc=(ch == 7))
        p.wait("sync", p.ve)
        for ch in range(8):
            p.dma(xr_sb[:, ch * W:(ch + 1) * W], rso[ch * 128:(ch + 1) * 128, ct:ct + W])
        p.wait("vector", p.ld)
        for ch in range(8):
            sl = slice(ch * W, (ch + 1) * W)
            p.vop("tensor_add", x2_sb[:, sl], x2_sb[:, sl], xr_sb[:, sl], inc=(ch == 7))
        p.wait("sync", p.ve)
        for ch in range(8):
            p.dma(g["x2_sp"][128 * ch:(ch + 1) * 128, t0:t0 + W], x2_sb[:, ch * W:(ch + 1) * W])
        ln_stats(x2_sb, W, LN_EPS)
        for ch in range(8):
            sl = slice(ch * W, (ch + 1) * W)
            p.vop("tensor_sub", xr_sb[:, sl], x2_sb[:, sl], mu_rep[:, 0:W], inc=False)
            p.vop("tensor_mul", xr_sb[:, sl], xr_sb[:, sl], rho_rep[:, 0:W], inc=(ch == 7))
        p.wait("sync", p.ve)
        for ch in range(8):
            p.dma(g["xf_sp"][128 * ch:(ch + 1) * 128, t0:t0 + W], xr_sb[:, ch * W:(ch + 1) * W])
    if "t_x2" in g:
        p.wait("sync", p.ld)
        p.dma(g["t_x2"][:, :], g["x2_sp"][:, :])

    if phases in ("attn", "full"):
        p.barrier()
        al.free("xr_sb")
        build_attn(nc, p, g, al, vec, pp, ppg, guard, ln_stats, locals())
    if phases == "full":
        p.barrier()
        build_moe(nc, p, g, al, vec, pp, ppg, guard, ln_stats, locals())

    p.wait("sync", p.ld)
    p.wait("sync", p.gd)
    p.wait("sync", p.st)

    with nc.Block() as block:
        @block.sync
        def _(e):
            for fn in p.streams["sync"]:
                fn(e)

        @block.gpsimd
        def _(e):
            for fn in p.streams["gpsimd"]:
                fn(e)

        @block.scalar
        def _(e):
            for fn in p.streams["scalar"]:
                fn(e)

        @block.vector
        def _(e):
            for fn in p.streams["vector"]:
                fn(e)

        @block.tensor
        def _(e):
            for fn in p.streams["tensor"]:
                fn(e)

    return nc


def build_attn(nc, p, g, al, vec, pp, ppg, guard, ln_stats, env):
    x_sb = env["x_sb"]; sq_sb = env["sq_sb"]; row_sb = env["row_sb"]
    t32a = env["t32a"]; ones32 = env["ones32"]; onesr = env["onesr"]
    onesbf = env["onesbf"]
    idn_sb = env["idn_sb"]; idn32_sb = env["idn32_sb"]
    rho_rep = env["rho_rep"]; x2_sb = env["x2_sb"]

    ao_sb = al.sb("ao_sb", [128, 8 * TLOC], BF16)
    q_sb = al.sb("q_sb", [128, 8 * TLOC], BF16)
    kc_sb = al.sb("kc_sb", [128, 2 * TLOC], BF16)
    vc_sb = al.sb("vc_sb", [128, 2 * TLOC], BF16)
    vt_bf = al.sb("vt_bf", [128, 256], BF16)
    wq_sb = al.sb("wq_sb", [128, 8 * D], BF16)
    for ch in range(8):
        p.dma(wq_sb[:, ch * D:(ch + 1) * D], g["wq_t"][128 * ch:(ch + 1) * 128, :],
              eng="gpsimd")
    wkv_sb = al.sb("wkv_sb", [128, 8 * 512], BF16)
    for ch in range(8):
        p.dma(wkv_sb[:, ch * 512:ch * 512 + 256], g["wkc_t"][128 * ch:(ch + 1) * 128, :],
              eng="gpsimd")
        p.dma(wkv_sb[:, ch * 512 + 256:(ch + 1) * 512],
              g["wvc_t"][128 * ch:(ch + 1) * 128, :], eng="gpsimd")
    gd_w = p.gd.n

    nx_sb = al.sb("nx_sb", [128, 8 * TLOC], BF16)
    for kt in range(4):
        t0 = kt * W
        p.wait("sync", p.ve); p.wait("sync", p.sc); p.wait("sync", p.mm)
        for ch in range(8):
            p.dma(x_sb[:, ch * W:(ch + 1) * W], g["xf_sp"][128 * ch:(ch + 1) * 128, t0:t0 + W])
        ln_stats(x_sb, W, RMS_EPS, rms_only=True)
        for ch in range(8):
            p.vop("tensor_mul", t32a[:, 0:W], x_sb[:, ch * W:(ch + 1) * W],
                  rho_rep[:, 0:W], inc=False)
            p.vop("tensor_scalar", nx_sb[:, ch * TLOC + t0:ch * TLOC + t0 + W],
                  t32a[:, 0:W], vec["g1v"][:, ch:ch + 1], None, OP.mult, inc=(ch == 7))
    if "t_nx1" in g:
        p.wait("sync", p.ve)
        for ch in range(8):
            p.dma(g["t_nx1"][128 * ch:(ch + 1) * 128, :], nx_sb[:, ch * TLOC:(ch + 1) * TLOC])

    p.wait("tensor", p.gd, gd_w)
    for kt in range(4):
        t0 = kt * W
        for qb in range(8):
            q = qb % 2
            guard(q)
            for ch in range(8):
                p.mmul(pp[q][:, 0:W], wq_sb[:, ch * D + qb * 128:ch * D + qb * 128 + 128],
                       nx_sb[:, ch * TLOC + t0:ch * TLOC + t0 + W], ch == 0, ch == 7)
            p.wait("scalar", p.mm)
            p.sop("activation", q_sb[:, qb * TLOC + t0:qb * TLOC + t0 + W],
                  pp[q][:, 0:W], AF.Copy)
            ppg[q].append((p.sc, p.sc.n))
        for kb in range(2):
            for which, dst in ((0, kc_sb), (256, vc_sb)):
                q = 2 + kb + (2 if which else 0)
                guard(q)
                for ch in range(8):
                    p.mmul(pp[q][:, 0:W],
                           wkv_sb[:, ch * 512 + which + kb * 128:ch * 512 + which + kb * 128 + 128],
                           nx_sb[:, ch * TLOC + t0:ch * TLOC + t0 + W], ch == 0, ch == 7)
                p.wait("scalar", p.mm)
                p.sop("activation", dst[:, kb * TLOC + t0:kb * TLOC + t0 + W],
                      pp[q][:, 0:W], AF.Copy)
                ppg[q].append((p.sc, p.sc.n))
    p.barrier()
    al.free("nx_sb")
    p.wait("sync", p.sc)
    for kb in range(2):
        p.dma(g["kc_snd"][kb * 128:(kb + 1) * 128, :], kc_sb[:, kb * TLOC:(kb + 1) * TLOC])
    for tt in range(16):
        for kb in range(2):
            q = 6
            guard(q)
            p.wait("vector", p.sc)
            p.wait("vector", p.mm)
            p.vop("tensor_copy", t32a[:, 0:128],
                  vc_sb[:, kb * TLOC + tt * 128:kb * TLOC + (tt + 1) * 128])
            p.wait("tensor", p.ve)
            p.emit("tensor", "transpose", pp[q][:, 0:128], t32a[:, 0:128],
                   idn32_sb[:, :], inc=(p.mm, 1))
            p.wait("scalar", p.mm)
            p.wait("scalar", p.ld)
            p.sop("activation", vt_bf[:, kb * 128:(kb + 1) * 128], pp[q][:, 0:128], AF.Copy)
            ppg[q].append((p.sc, p.sc.n))
        p.wait("sync", p.sc)
        p.dma(g["vct_snd"][tt * 128:(tt + 1) * 128, :], vt_bf[:, :])
    p.wait("gpsimd", p.ld)
    for nm_i, nm_o in (("kc_snd", "ag_kc"), ("vct_snd", "ag_vc")):
        p.emit("gpsimd", "collective_compute", "AllGather", OP.bypass,
               replica_groups=GROUPS, ins=[g[nm_i][:, :]], outs=[g[nm_o][:, :]],
               inc=(p.cc, 1))
    kv_cc = p.cc.n
    p.barrier()
    al.free("wq_sb", "wkv_sb", "vt_bf", "vc_sb", "kc_sb")

    if "t_vc" in g:
        p.wait("sync", p.cc)
        p.dma(g["t_vc"][:, :], g["ag_vc"][0:256, :])
    kcg_sb = al.sb("kcg_sb", [128, 2 * 4096], BF16)
    vcg_sb = al.sb("vcg_sb", [128, 32 * 256], BF16)
    p.wait("sync", p.cc, kv_cc)
    for kb in range(2):
        for half in range(2):
            p.dma(kcg_sb[:, kb * 4096 + half * TLOC:kb * 4096 + (half + 1) * TLOC],
                  g["ag_kc"][half * 256 + kb * 128:half * 256 + (kb + 1) * 128, :])
    for tb in range(32):
        p.dma(vcg_sb[:, tb * 256:(tb + 1) * 256], g["ag_vc"][tb * 128:(tb + 1) * 128, :])
    kv_ld = p.ld.n

    kn_row = al.sb("kn_row", [1, 16], F32)
    p.wait("scalar", p.ld, kv_ld)
    for seg in range(8):
        guard(7)
        for kb in range(2):
            sq = sq_sb[:, kb * W:(kb + 1) * W]
            p.wait("scalar", p.mm)
            p.sop("activation", sq,
                  kcg_sb[:, kb * 4096 + seg * W:kb * 4096 + (seg + 1) * W], AF.Square)
            p.wait("tensor", p.sc)
            p.mmul(pp[7][0:1, 0:W], ones32[:, :], sq, kb == 0, kb == 1, inc_force=True)
        p.wait("vector", p.mm)
        p.vop("reduce_max", kn_row[:, seg:seg + 1], pp[7][0:1, 0:W], AX)
        ppg[7].append((p.ve, p.ve.n))
    p.vop("reduce_max", kn_row[:, 8:9], kn_row[:, 0:8], AX)
    p.wait("vector", p.ve)

    bq_row = al.sb("bq_row", [1, TLOC], F32)
    brep_b = al.sb("brep_b", [128, TLOC], BF16)
    sc_f = al.sb("sc_f", [128, W], F32)
    prob_bf = al.sb("prob_bf", [128, 2 * W], BF16)
    sum_row = al.sb("sum_row", [1, TLOC], F32)
    rec_rep = al.sb("rec_rep", [128, TLOC], BF16)

    p.wait("tensor", p.ld, kv_ld)
    p.wait("tensor", p.sc)
    for hd in range(4):
        for seg in range(4):
            guard(7)
            p.wait("scalar", p.ve)
            for kb in range(2):
                sq = sq_sb[:, kb * W:(kb + 1) * W]
                p.wait("scalar", p.mm)
                p.sop("activation", sq,
                      q_sb[:, (2 * hd + kb) * TLOC + seg * W:(2 * hd + kb) * TLOC + (seg + 1) * W],
                      AF.Square)
                p.wait("tensor", p.sc)
                p.mmul(pp[7][0:1, 0:W], ones32[:, :], sq, kb == 0, kb == 1, inc_force=True)
            p.wait("vector", p.mm)
            p.vop("tensor_copy", bq_row[:, seg * W:(seg + 1) * W], pp[7][0:1, 0:W])
            ppg[7].append((p.ve, p.ve.n))
        p.vop("tensor_scalar", bq_row[:, :], bq_row[:, :], kn_row[0:1, 8:9], None,
              OP.mult)
        p.wait("scalar", p.ve)
        p.sop("activation", bq_row[:, :], bq_row[:, :], AF.Sqrt)
        p.wait("vector", p.sc)
        p.vop("tensor_scalar_mul", bq_row[:, :], bq_row[:, :], 1.0 / 16.0)
        p.wait("gpsimd", p.ve)
        p.emit("gpsimd", "dma_start", inc=(p.gd, 16), out=g["rowbf"][0:1, 0:TLOC],
               in_=bq_row[0:1, :])
        p.wait("gpsimd", p.gd)
        p.dma(brep_b[:, :], g["rowbf"][0:1, 0:TLOC].broadcast_to([128, TLOC]),
              eng="gpsimd")
        bq_n = p.gd.n

        for tq in range(4):
            t0 = tq * W
            guard(2); guard(3)
            guard(6)
            for tk in range(32):
                q = tk % 2
                guard(q)
                for kb in range(2):
                    p.mmul(pp[q][:, 0:W],
                           kcg_sb[:, kb * 4096 + tk * 128:kb * 4096 + (tk + 1) * 128],
                           q_sb[:, (2 * hd + kb) * TLOC + t0:(2 * hd + kb) * TLOC + t0 + W],
                           kb == 0, kb == 1)
                p.wait("vector", p.mm)
                p.wait("vector", p.gd, bq_n)
                p.wait("vector", p.sc)
                p.vop("scalar_tensor_tensor", sc_f[:, 0:W], pp[q][:, 0:W], 1.0 / 16.0,
                      brep_b[:, t0:t0 + W], OP.mult, OP.subtract)
                ppg[q].append((p.ve, p.ve.n))
                if hd == 0 and tq == 0 and tk == 0 and "t_attf" in g:
                    p.wait("sync", p.ve)
                    p.dma(g["t_attf"][:, 0:W], sc_f[:, 0:W])
                    p.dma(g["t_attf"][0:1, W:W + 1024], bq_row[0:1, 0:1024])
                    p.dma(g["t_attb"][:, 0:W], brep_b[:, 0:W])
                    p.wait("vector", p.ld)
                    p.wait("scalar", p.ld)
                p.wait("scalar", p.ve)
                pslot = prob_bf[:, (tk % 2) * W:(tk % 2) * W + W]
                p.wait("scalar", p.mm)
                p.sop("activation", pslot, sc_f[:, 0:W], AF.Exp)
                if hd == 0 and tq == 0 and tk == 0 and "t_attb" in g:
                    p.wait("sync", p.sc)
                    p.dma(g["t_attb"][:, W:2 * W], pslot)
                    p.wait("tensor", p.ld)
                    p.wait("scalar", p.ld)
                p.wait("tensor", p.sc)
                p.mmul(pp[6][:, 0:W], onesbf[:, :], pslot, tk == 0, tk == 31,
                       inc_last=False)
                for vb in range(2):
                    p.mmul(pp[2 + vb][:, 0:W],
                           vcg_sb[:, tk * 256 + vb * 128:tk * 256 + (vb + 1) * 128],
                           pslot, tk == 0, tk == 31, inc_last=(vb == 1))
            p.wait("vector", p.mm)
            p.vop("tensor_copy", sum_row[:, t0:t0 + W], pp[6][0:1, 0:W], inc=True)
            if hd == 0 and tq == 0 and "t_attf" in g:
                p.wait("sync", p.ve)
                p.emit("sync", "dma_start", inc=(p.ld, 16),
                       out=g["t_attf"][1:2, W:W + W], in_=sum_row[0:1, 0:W])
                p.vop("tensor_copy", t32a[:, 0:W], pp[2][:, 0:W])
                p.wait("sync", p.ve)
                p.emit("sync", "dma_start", inc=(p.ld, 16),
                       out=g["t_attf"][:, 2 * W:3 * W], in_=t32a[:, 0:W])
                p.wait("vector", p.ld)
            p.vop("reciprocal", sum_row[:, t0:t0 + W], sum_row[:, t0:t0 + W])
            ppg[6].append((p.ve, p.ve.n))
            p.wait("gpsimd", p.ve)
            p.emit("gpsimd", "dma_start", inc=(p.gd, 16),
                   out=g["rowbf"][0:1, TLOC + t0:TLOC + t0 + W], in_=sum_row[0:1, t0:t0 + W])
            p.wait("gpsimd", p.gd)
            p.dma(rec_rep[:, t0:t0 + W],
                  g["rowbf"][0:1, TLOC + t0:TLOC + t0 + W].broadcast_to([128, W]),
                  eng="gpsimd")
            p.wait("vector", p.gd)
            for vb in range(2):
                p.vop("tensor_mul",
                      ao_sb[:, (2 * hd + vb) * TLOC + t0:(2 * hd + vb) * TLOC + t0 + W],
                      rec_rep[:, t0:t0 + W], pp[2 + vb][:, 0:W])
                ppg[2 + vb].append((p.ve, p.ve.n))
    p.barrier()
    al.free("kcg_sb", "vcg_sb", "kn_row", "bq_row", "brep_b", "sc_f", "prob_bf",
            "sum_row", "rec_rep", "q_sb")
    if "t_ao" in g:
        aot = al.sb("aot", [128, W], F32)
        for ch in range(8):
            for kt in range(4):
                p.wait("vector", p.ld)
                p.vop("tensor_copy", aot[:, 0:W],
                      ao_sb[:, ch * TLOC + kt * W:ch * TLOC + (kt + 1) * W])
                p.wait("sync", p.ve)
                p.dma(g["t_ao"][128 * ch:(ch + 1) * 128, kt * W:(kt + 1) * W], aot[:, 0:W])
        al.free("aot")

    wo_sb2 = al.sb("wo_sb2", [128, 8 * D], BF16)
    for ch in range(8):
        p.dma(wo_sb2[:, ch * D:(ch + 1) * D], g["wo_t"][128 * ch:(ch + 1) * 128, :],
              eng="gpsimd")
    gd_wo2 = p.gd.n
    p.wait("tensor", p.gd, gd_wo2)
    p.wait("tensor", p.ve)
    for kt in range(4):
        t0 = kt * W
        p.wait("sync", p.ve)
        for ch in range(8):
            p.dma(x_sb[:, ch * W:(ch + 1) * W], g["xf_sp"][128 * ch:(ch + 1) * 128, t0:t0 + W])
        for ob in range(8):
            q = ob % 2
            guard(q)
            for ch in range(8):
                p.mmul(pp[q][:, 0:W], wo_sb2[:, ch * D + ob * 128:ch * D + ob * 128 + 128],
                       ao_sb[:, ch * TLOC + t0:ch * TLOC + t0 + W], ch == 0, ch == 7)
            p.wait("vector", p.mm)
            p.wait("vector", p.ld)
            p.vop("tensor_add", x2_sb[:, ob * W:(ob + 1) * W], pp[q][:, 0:W],
                  x_sb[:, ob * W:(ob + 1) * W])
            ppg[q].append((p.ve, p.ve.n))
            p.wait("sync", p.ve)
            p.dma(g["x3_sp"][128 * ob:(ob + 1) * 128, t0:t0 + W],
                  x2_sb[:, ob * W:(ob + 1) * W])
    if "t_x3" in g:
        p.wait("sync", p.ld)
        p.dma(g["t_x3"][:, :], g["x3_sp"][:, :])
    p.barrier()
    al.free("wo_sb2", "ao_sb")


def build_moe(nc, p, g, al, vec, pp, ppg, guard, ln_stats, env):
    x_sb = env["x_sb"]; row_sb = env["row_sb"]; t32a = env["t32a"]; t32b = env["t32b"]
    ones32 = env["ones32"]; idn32_sb = env["idn32_sb"]; rho_rep = env["rho_rep"]
    x2_sb = env["x2_sb"]
    xf2_sb = al.sb("xf2_sb", [128, 8 * TLOC], BF16)
    wr_sb = al.sb("wr_sb", [128, 8 * 8], BF16)
    for ch in range(8):
        p.dma(wr_sb[:, ch * 8:(ch + 1) * 8], g["wr_t"][128 * ch:(ch + 1) * 128, :],
              eng="gpsimd")
    gd_wr = p.gd.n

    for kt in range(4):
        t0 = kt * W
        p.wait("sync", p.ve); p.wait("sync", p.sc); p.wait("sync", p.mm)
        for ch in range(8):
            p.dma(x_sb[:, ch * W:(ch + 1) * W], g["x3_sp"][128 * ch:(ch + 1) * 128, t0:t0 + W])
        ln_stats(x_sb, W, RMS_EPS, rms_only=True)
        for ch in range(8):
            p.vop("tensor_mul", t32a[:, 0:W], x_sb[:, ch * W:(ch + 1) * W],
                  rho_rep[:, 0:W], inc=False)
            p.vop("tensor_scalar", xf2_sb[:, ch * TLOC + t0:ch * TLOC + t0 + W],
                  t32a[:, 0:W], vec["g2v"][:, ch:ch + 1], None, OP.mult, inc=(ch == 7))

    lgv = al.sb("lgv", [128, 5 * 128 + 3 * 16], F32)
    lgs = al.sb("lgs", [128, 128 + 16], F32)
    cmb = al.sb("cmb", [128, 128], F32)
    LOGa = 0; LMa = 128; RWa = 256; MKa = 384; SSa = 512
    M1a = 640; M2a = 656; RZa = 672
    ELa = 0; ZZa = 128

    p.wait("tensor", p.gd, gd_wr)
    for tt in range(16):
        t0 = tt * 128
        q = 6 if tt % 2 == 0 else 7
        guard(q)
        p.wait("tensor", p.ve)
        for ch in range(8):
            p.mmul(pp[q][0:128, 0:8], xf2_sb[:, ch * TLOC + t0:ch * TLOC + t0 + 128],
                   wr_sb[:, ch * 8:(ch + 1) * 8], ch == 0, ch == 7)
        p.wait("vector", p.mm)
        p.vop("tensor_add", lgv[:, LOGa + 8 * tt:LOGa + 8 * tt + 8],
              pp[q][0:128, 0:8], vec["ebias"][:, 0:8])
        ppg[q].append((p.ve, p.ve.n))

    def bc8(col):
        return lgv[:, col:col + 1].broadcast_to([128, 8])

    for tt in range(16):
        p.vop("reduce_max", lgv[:, M1a + tt:M1a + tt + 1],
              lgv[:, LOGa + 8 * tt:LOGa + 8 * tt + 8], AX, inc=(tt == 15))
    for tt in range(16):
        p.vop("tensor_tensor", lgv[:, LMa + 8 * tt:LMa + 8 * tt + 8],
              lgv[:, LOGa + 8 * tt:LOGa + 8 * tt + 8], bc8(M1a + tt),
              OP.subtract, inc=(tt == 15))
    p.wait("scalar", p.ve)
    for tt in range(16):
        p.sop("activation", lgs[:, ELa + 8 * tt:ELa + 8 * tt + 8],
              lgv[:, LMa + 8 * tt:LMa + 8 * tt + 8], AF.Exp,
              accum_out=lgs[:, ZZa + tt:ZZa + tt + 1], inc=(tt == 15))
    p.wait("vector", p.sc)
    p.vop("reciprocal", lgv[:, RZa:RZa + 16], lgs[:, ZZa:ZZa + 16])
    p.vop("tensor_copy", lgv[:, M2a:M2a + 16], lgv[:, RZa:RZa + 16])
    p.vop("tensor_copy", lgv[:, M2a:M2a + 16], lgv[:, RZa:RZa + 16])
    p.vop("tensor_copy", lgv[:, M2a:M2a + 16], lgv[:, RZa:RZa + 16])
    for tt in range(16):
        p.vop("tensor_tensor", lgv[:, RWa + 8 * tt:RWa + 8 * tt + 8],
              lgs[:, ELa + 8 * tt:ELa + 8 * tt + 8],
              lgv[:, RZa + tt:RZa + tt + 1].broadcast_to([128, 8]), OP.mult,
              inc=(tt == 15))
    for tt in range(16):
        p.vop("reduce_max", lgv[:, M1a + tt:M1a + tt + 1],
              lgv[:, RWa + 8 * tt:RWa + 8 * tt + 8], AX, inc=(tt == 15))
    for tt in range(16):
        p.vop("tensor_tensor", lgv[:, MKa + 8 * tt:MKa + 8 * tt + 8],
              lgv[:, RWa + 8 * tt:RWa + 8 * tt + 8], bc8(M1a + tt), OP.is_ge,
              inc=(tt == 15))
    p.vop("scalar_tensor_tensor", lgv[:, MKa:MKa + 128], lgv[:, MKa:MKa + 128],
          -1e30, lgv[:, RWa:RWa + 128], OP.mult, OP.add)
    for tt in range(16):
        p.vop("reduce_max", lgv[:, M2a + tt:M2a + tt + 1],
              lgv[:, MKa + 8 * tt:MKa + 8 * tt + 8], AX, inc=(tt == 15))
    for tt in range(16):
        p.vop("tensor_tensor", lgv[:, MKa + 8 * tt:MKa + 8 * tt + 8],
              lgv[:, RWa + 8 * tt:RWa + 8 * tt + 8], bc8(M2a + tt), OP.is_ge,
              inc=(tt == 15))
    p.vop("scalar_tensor_tensor", lgv[:, SSa:SSa + 128], lgv[:, MKa:MKa + 128],
          100.0, lgv[:, RWa:RWa + 128], OP.mult, OP.add)
    for tt in range(16):
        p.vop("reduce_max", lgv[:, M1a + tt:M1a + tt + 1],
              lgv[:, SSa + 8 * tt:SSa + 8 * tt + 8], AX, inc=(tt == 15))
    for tt in range(16):
        p.vop("tensor_tensor", lgv[:, LMa + 8 * tt:LMa + 8 * tt + 8],
              lgv[:, SSa + 8 * tt:SSa + 8 * tt + 8], bc8(M1a + tt), OP.subtract,
              inc=(tt == 15))
    p.wait("scalar", p.ve)
    for tt in range(16):
        p.sop("activation", lgs[:, ELa + 8 * tt:ELa + 8 * tt + 8],
              lgv[:, LMa + 8 * tt:LMa + 8 * tt + 8], AF.Exp,
              accum_out=lgs[:, ZZa + tt:ZZa + tt + 1], inc=(tt == 15))
    p.wait("vector", p.sc)
    p.vop("reciprocal", lgv[:, RZa:RZa + 16], lgs[:, ZZa:ZZa + 16])
    p.vop("tensor_copy", lgv[:, M2a:M2a + 16], lgv[:, RZa:RZa + 16])
    p.vop("tensor_copy", lgv[:, M2a:M2a + 16], lgv[:, RZa:RZa + 16])
    p.vop("tensor_copy", lgv[:, M2a:M2a + 16], lgv[:, RZa:RZa + 16])
    for tt in range(16):
        p.vop("tensor_tensor", cmb[:, 8 * tt:8 * tt + 8],
              lgs[:, ELa + 8 * tt:ELa + 8 * tt + 8],
              lgv[:, RZa + tt:RZa + tt + 1].broadcast_to([128, 8]), OP.mult,
              inc=(tt == 15))
    for tt in range(16):
        t0 = tt * 128
        q2 = 4
        guard(q2)
        p.wait("tensor", p.ve)
        p.emit("tensor", "transpose", pp[q2][0:8, 0:128], cmb[:, 8 * tt:8 * tt + 8],
               idn32_sb[:, :], inc=(p.mm, 1))
        p.wait("vector", p.mm)
        p.wait("vector", p.ld)
        p.vop("tensor_copy", t32a[0:8, 0:128], pp[q2][0:8, 0:128])
        ppg[q2].append((p.ve, p.ve.n))
        p.wait("sync", p.ve)
        p.dma(g["comb_fm"][0:8, t0:t0 + 128], t32a[0:8, 0:128])
    p.wait("vector", p.ld)
    p.vop("memset", t32a[0:1, 0:W], 1.0)
    p.wait("sync", p.ve)
    for kt in range(4):
        p.dma(g["comb_fm"][8:9, kt * W:(kt + 1) * W], t32a[0:1, 0:W])
    if "t_comb" in g:
        p.wait("sync", p.ld)
        p.dma(g["t_comb"][:, :], g["comb_fm"][:, :])
    p.barrier()
    al.free("cmb", "lgs", "lgv", "wr_sb")

    we1 = al.sb("we1", [128, 8 * DFF], BF16)
    we3 = al.sb("we3", [128, 8 * DFF], BF16)
    we2 = al.sb("we2", [128, 8 * D], BF16)
    h1_bf = al.sb("h1_bf", [128, 8 * W], BF16)
    hp_bf = al.sb("hp_bf", [128, 8 * W], BF16)
    crep_f = al.sb("crep_f", [128, W], F32)
    acc_sb = al.sb("acc_sb", [128, 8 * TCW], F32)

    for th in range(2):
        p.wait("vector", p.mm)
        p.vop("memset", acc_sb[:, :], 0.0)
        for ex in range(9):
            p.wait("gpsimd", p.mm)
            for ch in range(8):
                if ex < 8:
                    p.dma(we1[:, ch * DFF:(ch + 1) * DFF],
                          g["ew1_t"][ex, 128 * ch:(ch + 1) * 128, :], eng="gpsimd")
                    p.dma(we3[:, ch * DFF:(ch + 1) * DFF],
                          g["ew3_t"][ex, 128 * ch:(ch + 1) * 128, :], eng="gpsimd")
                    p.dma(we2[:, ch * D:(ch + 1) * D],
                          g["ew2_t"][ex, 128 * ch:(ch + 1) * 128, :], eng="gpsimd")
                else:
                    p.dma(we1[:, ch * DFF:(ch + 1) * DFF],
                          g["sw1_t"][128 * ch:(ch + 1) * 128, :], eng="gpsimd")
                    p.dma(we3[:, ch * DFF:(ch + 1) * DFF],
                          g["sw3_t"][128 * ch:(ch + 1) * 128, :], eng="gpsimd")
                    p.dma(we2[:, ch * D:(ch + 1) * D],
                          g["sw2_t"][128 * ch:(ch + 1) * 128, :], eng="gpsimd")
            gd_e = p.gd.n
            p.wait("tensor", p.gd, gd_e)
            for kt in range(2):
                t0 = th * TCW + kt * W
                ta = kt * W
                p.wait("gpsimd", p.ve)
                p.dma(crep_f[:, :], g["comb_fm"][ex:ex + 1, t0:t0 + W]
                      .broadcast_to([128, W]), eng="gpsimd")
                crep_n = p.gd.n
                p.wait("vector", p.mm)
                for fb in range(8):
                    q = fb % 2
                    q2 = 2 + fb % 2
                    guard(q); guard(q2)
                    for ch in range(8):
                        p.mmul(pp[q][:, 0:W],
                               we1[:, ch * DFF + fb * 128:ch * DFF + fb * 128 + 128],
                               xf2_sb[:, ch * TLOC + t0:ch * TLOC + t0 + W],
                               ch == 0, ch == 7, inc_last=False)
                        p.mmul(pp[q2][:, 0:W],
                               we3[:, ch * DFF + fb * 128:ch * DFF + fb * 128 + 128],
                               xf2_sb[:, ch * TLOC + t0:ch * TLOC + t0 + W],
                               ch == 0, ch == 7)
                    p.wait("scalar", p.mm)
                    p.wait("scalar", p.ve)
                    p.sop("activation", h1_bf[:, fb * W:(fb + 1) * W], pp[q][:, 0:W],
                          AF.Silu)
                    ppg[q].append((p.sc, p.sc.n))
                    p.wait("vector", p.sc)
                    p.wait("vector", p.gd, crep_n)
                    p.vop("tensor_mul", t32b[:, 0:W], crep_f[:, 0:W], pp[q2][:, 0:W],
                          inc=False)
                    p.vop("tensor_mul", hp_bf[:, fb * W:(fb + 1) * W],
                          h1_bf[:, fb * W:(fb + 1) * W], t32b[:, 0:W])
                    ppg[q2].append((p.ve, p.ve.n))
                for ob in range(8):
                    q = 4 + ob % 2
                    guard(q)
                    p.wait("tensor", p.ve)
                    for fb in range(8):
                        p.mmul(pp[q][:, 0:W],
                               we2[:, fb * D + ob * 128:fb * D + ob * 128 + 128],
                               hp_bf[:, fb * W:(fb + 1) * W], fb == 0, fb == 7)
                    p.wait("vector", p.mm)
                    p.vop("tensor_add", acc_sb[:, ob * TCW + ta:ob * TCW + ta + W],
                          acc_sb[:, ob * TCW + ta:ob * TCW + ta + W], pp[q][:, 0:W])
                    ppg[q].append((p.ve, p.ve.n))
        for kt in range(2):
            t0 = th * TCW + kt * W
            ta = kt * W
            p.wait("sync", p.ve)
            for ch in range(8):
                p.dma(x_sb[:, ch * W:(ch + 1) * W],
                      g["x2_sp"][128 * ch:(ch + 1) * 128, t0:t0 + W])
            for ch in range(8):
                sl = slice(ch * W, (ch + 1) * W)
                p.dma(t32b[:, 0:W], g["x3_sp"][128 * ch:(ch + 1) * 128, t0:t0 + W])
                p.wait("vector", p.ld)
                p.vop("tensor_add", t32a[:, 0:W], x_sb[:, sl], t32b[:, 0:W], inc=False)
                p.vop("tensor_add", x2_sb[:, sl], t32a[:, 0:W],
                      acc_sb[:, ch * TCW + ta:ch * TCW + ta + W], inc=True)
                p.wait("sync", p.ve)
                p.emit("sync", "dma_start", inc=(p.st, 16),
                       out=g["out_fm"][128 * ch:(ch + 1) * 128, t0:t0 + W],
                       in_=x2_sb[:, ch * W:(ch + 1) * W])


def run(inputs, taps=(), phases="full"):
    in_maps = host_prepare(inputs)
    nc = build_nc(taps=taps, phases=phases)
    res = run_bass_kernel_spmd(nc, in_maps, core_ids=list(range(8)))
    return res


def kernel(**inputs):
    res = run(inputs, taps=(), phases="full")
    return host_finalize(res.results)


# version 02
# speedup vs baseline: 1.1773x; 1.1773x over previous
import numpy as np
import ml_dtypes
import concourse.bass as bass
import concourse.mybir as mybir
from concourse.bass import IndirectOffsetOnAxis
from concourse.bass_utils import run_bass_kernel_spmd

F32 = mybir.dt.float32
BF16 = mybir.dt.bfloat16
I32 = mybir.dt.int32
AF = mybir.ActivationFunctionType
OP = mybir.AluOpType
AX = mybir.AxisListType.X

D = 1024
DI = 2048
DS = 16
DTR = 64
TLOC = 2048
HALO = 3
TW = 2051
W = 512
TCW = 1024
CH_HALF = 1024
NHC = 8
SND_R = CH_HALF + 128
AG_R = 2 * SND_R
NEXP = 8
DFF = 1024
LN_EPS = 1e-5
RMS_EPS = 1e-6

GROUPS = [[0, 1], [2, 3], [4, 5], [6, 7]]
ENGINES = ("sync", "gpsimd", "scalar", "vector", "tensor")


def host_prepare(inp):
    f = np.ascontiguousarray
    x = np.asarray(inp["L0_attn_x"], np.float32)

    def chvec(v, n):
        return f(np.asarray(v, np.float32).reshape(n, 128).T)

    shared = {
        "mg": chvec(inp["L0_attn_norm_g"], 8),
        "mb": chvec(inp["L0_attn_norm_b"], 8),
        "cb": chvec(inp["L0_attn_conv_b"], 16),
        "cw": f(np.asarray(inp["L0_attn_conv_w"], np.float32).reshape(16, 128, 4)
                .transpose(1, 0, 2).reshape(128, 64)),
        "dtb": chvec(inp["L0_attn_dt_proj_b"], 16),
        "g1v": chvec(inp["L0_ffn_g1"], 8),
        "g2v": chvec(inp["L0_ffn_g2"], 8),
        "ebias": f(np.broadcast_to(np.asarray(inp["L0_ffn_expert_bias"], np.float32)[None, :],
                                   (128, 8)).copy()),
        "wip_t": f(np.asarray(inp["L0_attn_in_proj_w"], np.float32).T),
        "wxp_t": f(np.asarray(inp["L0_attn_x_proj_w"], np.float32).T),
        "wdt_t": f(np.asarray(inp["L0_attn_dt_proj_w"], np.float32).T),
        "wq_t": f(np.asarray(inp["L0_ffn_Wq"], np.float32).T),
        "wkc_t": f(np.asarray(inp["L0_ffn_Wkc"], np.float32).T),
        "wvc_t": f(np.asarray(inp["L0_ffn_Wvc"], np.float32).T),
        "wo_t": f(np.asarray(inp["L0_ffn_Wo"], np.float32).T),
        "wr_t": f(np.asarray(inp["L0_ffn_Wr"], np.float32).T),
        "sw1_t": f(np.asarray(inp["L0_ffn_sW1"], np.float32).T),
        "sw2_t": f(np.asarray(inp["L0_ffn_sW2"], np.float32).T),
        "sw3_t": f(np.asarray(inp["L0_ffn_sW3"], np.float32).T),
        "ew1_t": f(np.asarray(inp["L0_ffn_eW1"], np.float32).transpose(0, 2, 1)),
        "ew2_t": f(np.asarray(inp["L0_ffn_eW2"], np.float32).transpose(0, 2, 1)),
        "ew3_t": f(np.asarray(inp["L0_ffn_eW3"], np.float32).transpose(0, 2, 1)),
        "idn": f(np.eye(128, dtype=ml_dtypes.bfloat16)),
        "idn32": f(np.eye(128, dtype=np.float32)),
    }
    A_full = np.asarray(inp["L0_attn_log_A"], np.float32)
    Dp_full = np.asarray(inp["L0_attn_Dp"], np.float32)
    wout = np.asarray(inp["L0_attn_out_proj_w"], np.float32)

    in_maps = []
    for c in range(8):
        b, h = c // 2, c % 2
        t0 = h * TLOC
        xw = np.zeros((TW, D), np.float32)
        lo = max(t0 - HALO, 0)
        xw[HALO - (t0 - lo):] = x[b, lo:t0 + TLOC]
        x_fm = f(xw.T)
        my = slice(h * CH_HALF, (h + 1) * CH_HALF)
        A_my = -np.exp(A_full[my])
        A_sc = f(A_my.reshape(8, 128, 16).transpose(1, 0, 2).reshape(128, 128))
        Dp_sc = chvec(Dp_full[my], 8)
        wout_t_half = f(wout.T[my])

        pr = np.arange(128, dtype=np.int32)
        cols = []
        for tc in range(4):
            for dch in range(NHC):
                cols.append(tc * DI + h * CH_HALF + dch * 128 + pr)
        idx_all = f(np.stack(cols, 1).astype(np.int32))

        m = {"x_fm": x_fm, "A_sc": A_sc, "Dp_sc": Dp_sc,
             "wout_t_half": wout_t_half, "idx_all": idx_all}
        m.update(shared)
        in_maps.append(m)
    return in_maps


def host_finalize(results):
    out = np.empty((4, 4096, 1024), np.float32)
    for c in range(8):
        b, h = c // 2, c % 2
        out[b, h * TLOC:(h + 1) * TLOC] = np.asarray(results[c]["out_fm"]).T
    return out


class Sem:
    def __init__(self, nc, name):
        self.h = nc.semaphore(name).__enter__()
        self.n = 0


class P:
    def __init__(self, nc):
        self.nc = nc
        self.streams = {e: [] for e in ENGINES}
        for s in ("ld", "gd", "mm", "ve", "sc", "cc", "st"):
            setattr(self, s, Sem(nc, "s_" + s))

    def barrier(self):
        for eng in ENGINES:
            for nm in ("ld", "gd", "mm", "ve", "sc", "cc"):
                self.wait(eng, getattr(self, nm))

    def wait(self, eng, sem, val=None):
        v = sem.n if val is None else val
        if v <= 0:
            return
        self.streams[eng].append(lambda e, s=sem.h, vv=v: e.wait_ge(s, vv))

    def emit(self, eng, meth, *args, inc=None, **kwargs):
        if inc is not None:
            sem, k = inc
            sem.n += k

            def fn(e, a=args, kw=kwargs, s=sem.h, kk=k, m=meth):
                getattr(e, m)(*a, **kw).then_inc(s, kk)
        else:
            def fn(e, a=args, kw=kwargs, m=meth):
                getattr(e, m)(*a, **kw)
        self.streams[eng].append(fn)

    def dma(self, out, in_, eng="sync", sem=None, **kw):
        if sem is None:
            sem = self.gd if eng == "gpsimd" else self.ld
        self.emit(eng, "dma_start", inc=(sem, 16), out=out, in_=in_, **kw)

    def scat(self, out, idx, in_):
        self.emit("gpsimd", "indirect_dma_start", inc=(self.gd, 16),
                  out=out, out_offset=IndirectOffsetOnAxis(ap=idx, axis=0),
                  in_=in_, in_offset=None)

    def fence(self, g, ones_src):
        self.emit("gpsimd", "dma_start", inc=(self.gd, 16),
                  out=g["fence_scr"][0:1, 0:1], in_=ones_src)

    def mmul(self, out, lhsT, rhs, first, last, inc_last=None, inc_force=False):
        if inc_last is None:
            inc_last = last
        self.emit("tensor", "matmul", out, lhsT, rhs, start=first, stop=last,
                  inc=(self.mm, 1) if (inc_last or inc_force) else None)

    def vop(self, meth, *args, inc=True, **kw):
        self.emit("vector", meth, *args, inc=(self.ve, 1) if inc else None, **kw)

    def sop(self, meth, *args, inc=True, **kw):
        self.emit("scalar", meth, *args, inc=(self.sc, 1) if inc else None, **kw)


class Alloc:
    def __init__(self, nc):
        self.nc = nc
        self.stack = []

    def sb(self, name, shape, dtype):
        cm = self.nc.sbuf_tensor(name, shape, dtype)
        t = cm.__enter__()
        self.stack.append((name, cm))
        return t

    def free(self, *names):
        want = set(names)
        top = [n for n, _ in self.stack[-len(want):]]
        assert set(top) == want, f"non-stack free: want {want}, top {top}"
        for _ in range(len(want)):
            n, cm = self.stack.pop()
            cm.__exit__(None, None, None)


def build_nc(taps=(), phases="full"):
    nc = bass.Bass("TRN2", target_bir_lowering=False, debug=False)
    g = {}
    al = Alloc(nc)

    def din(name, shape, dtype=F32):
        g[name] = nc.dram_tensor(name, shape, dtype, kind="ExternalInput").ap()

    def dout(name, shape, dtype=F32):
        g[name] = nc.dram_tensor(name, shape, dtype, kind="ExternalOutput").ap()

    def dtmp(name, shape, dtype=F32):
        g[name] = nc.dram_tensor(name, shape, dtype).ap()

    din("x_fm", [D, TW]); din("A_sc", [128, 128]); din("Dp_sc", [128, 8])
    din("wout_t_half", [CH_HALF, D]); din("idx_all", [128, 32], I32)
    for nm, wd_ in (("mg", 8), ("mb", 8), ("cb", 16), ("cw", 64), ("dtb", 16),
                    ("g1v", 8), ("g2v", 8), ("ebias", 8)):
        din(nm, [128, wd_])
    din("wip_t", [D, 2 * DI]); din("wxp_t", [DI, 96]); din("wdt_t", [DTR, DI])
    din("wq_t", [D, D]); din("wkc_t", [D, 256]); din("wvc_t", [D, 256])
    din("wo_t", [D, D]); din("wr_t", [D, 8])
    din("sw1_t", [D, DFF]); din("sw2_t", [DFF, D]); din("sw3_t", [D, DFF])
    din("ew1_t", [NEXP, D, DFF]); din("ew2_t", [NEXP, DFF, D]); din("ew3_t", [NEXP, D, DFF])
    din("idn", [128, 128], BF16); din("idn32", [128, 128])
    dout("out_fm", [D, TLOC])

    for nm in ("snd_d", "snd_xs", "snd_z"):
        dtmp(nm, [2 * DI, TCW], BF16)
    for nm in ("ag_d", "ag_xs", "ag_z"):
        dtmp(nm, [4 * DI, TCW], BF16)
    dtmp("bc_snd", [64, TCW], BF16)
    dtmp("ag_bc", [128, TCW], BF16)
    dtmp("rs_in_a", [2 * D, TCW]); dtmp("rs_in_b", [2 * D, TCW])
    dtmp("rs_out_a", [D, TCW]); dtmp("rs_out_b", [D, TCW])
    dtmp("xmu_sp", [D, TLOC])
    dtmp("x2_sp", [D, TLOC]); dtmp("xf_sp", [D, TLOC]); dtmp("x3_sp", [D, TLOC])
    dtmp("kc_snd", [256, TLOC], BF16); dtmp("ag_kc", [512, TLOC], BF16)
    dtmp("vct_snd", [TLOC, 256], BF16); dtmp("ag_vc", [2 * TLOC, 256], BF16)
    dtmp("rowb", [1, 4096])
    dtmp("fence_scr", [1, 64])
    dtmp("rowbf", [1, 4096], BF16)
    dtmp("comb_fm", [16, TLOC])

    tap_specs = {
        "t_xs": ([DI, TLOC], BF16), "t_zs": ([DI, TLOC], BF16),
        "t_delta": ([DI, TLOC], BF16), "t_bc": ([32, TLOC], BF16),
        "t_y": ([CH_HALF, 2 * TLOC], F32), "t_x2": ([D, TLOC], F32),
        "t_nx1": ([D, TLOC], BF16), "t_ao": ([D, TLOC], F32),
        "t_x3": ([D, TLOC], F32), "t_comb": ([16, TLOC], F32),
        "t_xmu": ([D, TLOC], F32), "t_r1": ([1, TLOC], F32),
        "t_xn": ([D, TW], BF16),
        "t_attf": ([128, 4 * W], F32),
        "t_vc": ([256, 256], BF16),
        "t_attb": ([128, 2 * W], BF16),
        "t_rows": ([1, 6 * W], F32),
        "t_xsp": ([128, TW], BF16),
        "t_cv": ([128, W], F32),
        "t_cvw": ([128, W + HALO], F32),
        "t_xg": ([128, W], BF16),
        "t_rhot": ([128, 16], F32),
    }
    for t in taps:
        dout(t, *tap_specs[t])

    p = P(nc)

    idx_sb = al.sb("idx_sb", [128, 32], I32)
    p.dma(idx_sb[:, :], g["idx_all"][:, :])
    IDX_SCAN = lambda tc, dch: idx_sb[:, tc * NHC + dch:tc * NHC + dch + 1]

    ones32 = al.sb("ones32", [128, 1], F32)
    onesbf = al.sb("onesbf", [128, 128], BF16)
    onesr = al.sb("onesr", [1, 128], F32)
    p.vop("memset", ones32[:, :], 1.0, inc=False)
    p.vop("memset", onesbf[:, :], 1.0, inc=False)
    p.vop("memset", onesr[:, :], 1.0)
    vec = {}
    for nm, wd_ in (("mg", 8), ("mb", 8), ("cb", 16), ("cw", 64), ("dtb", 16),
                    ("g1v", 8), ("g2v", 8), ("ebias", 8), ("A_sc", 128), ("Dp_sc", 8)):
        vec[nm] = al.sb("v_" + nm, [128, wd_], F32)
        p.dma(vec[nm][:, :], g[nm][:, :])
    idn_sb = al.sb("idn_sb", [128, 128], BF16)
    p.dma(idn_sb[:, :], g["idn"][:, :])
    idn32_sb = al.sb("idn32_sb", [128, 128], F32)
    p.dma(idn32_sb[:, :], g["idn32"][:, :])

    PPN = 8
    pp = []
    for i in range(PPN):
        cm = nc.psum_tensor(f"pp{i}", [128, 512], F32)
        pp.append(cm.__enter__())
    ppg = [[] for _ in range(PPN)]

    def guard(i):
        for s_, v_ in ppg[i]:
            p.wait("tensor", s_, v_)
        ppg[i] = []

    x_sb = al.sb("x_sb", [128, 8 * W], F32)
    sq_sb = al.sb("sq_sb", [128, 2 * W], F32)
    row_sb = al.sb("row_sb", [1, 6 * W], F32)
    mu_rep = al.sb("mu_rep", [128, W], F32)
    rho_rep = al.sb("rho_rep", [128, W], F32)
    t32a = al.sb("t32a", [128, W], F32)
    t32b = al.sb("t32b", [128, W], F32)
    x2_sb = al.sb("x2_sb", [128, 8 * W], F32)

    def ln_stats(src_sb, wd, eps, rms_only=False):
        guard(6); guard(7)
        p.wait("tensor", p.ve); p.wait("tensor", p.ld)
        if not rms_only:
            for ch in range(8):
                p.mmul(pp[6][0:1, 0:wd], ones32[:, :], src_sb[:, ch * W:ch * W + wd],
                       ch == 0, ch == 7)
        p.wait("scalar", p.ld); p.wait("scalar", p.ve)
        mm_at = []
        for ch in range(8):
            sq = sq_sb[:, (ch % 2) * W:(ch % 2) * W + wd]
            if ch >= 2:
                p.wait("scalar", p.mm, mm_at[ch - 2])
            p.sop("activation", sq, src_sb[:, ch * W:ch * W + wd], AF.Square)
            p.wait("tensor", p.sc)
            p.mmul(pp[7][0:1, 0:wd], ones32[:, :], sq, ch == 0, ch == 7, inc_force=True)
            mm_at.append(p.mm.n)
        p.wait("vector", p.mm)
        p.wait("vector", p.sc)
        R = row_sb
        mu, m2, var, t1, r1, srt = (R[:, i * W:i * W + wd] for i in range(6))
        if rms_only:
            p.vop("tensor_scalar", t1, pp[7][0:1, 0:wd], 1.0 / D, eps, OP.mult, OP.add,
                  inc=False)
            p.vop("reciprocal", r1, t1)
            ppg[7].append((p.ve, p.ve.n))
        else:
            p.vop("tensor_scalar_mul", mu, pp[6][0:1, 0:wd], 1.0 / D, inc=False)
            p.vop("tensor_mul", m2, mu, mu, inc=False)
            p.vop("scalar_tensor_tensor", var, pp[7][0:1, 0:wd], 1.0 / D, m2,
                  OP.mult, OP.subtract, inc=False)
            p.vop("tensor_scalar_add", t1, var, eps, inc=False)
            p.vop("reciprocal", r1, t1)
            ppg[6].append((p.ve, p.ve.n)); ppg[7].append((p.ve, p.ve.n))
        p.wait("scalar", p.ve)
        p.sop("activation", srt, r1, AF.Sqrt)
        guard(4)
        p.wait("tensor", p.sc)
        p.mmul(pp[4][:, 0:wd], onesr[:, :], srt, True, True)
        if not rms_only:
            guard(5)
            p.mmul(pp[5][:, 0:wd], onesr[:, :], mu, True, True)
        p.wait("vector", p.mm)
        p.vop("tensor_copy", rho_rep[:, 0:wd], pp[4][:, 0:wd], inc=not rms_only)
        ppg[4].append((p.ve, p.ve.n))
        if not rms_only:
            p.vop("tensor_copy", mu_rep[:, 0:wd], pp[5][:, 0:wd])
            ppg[5].append((p.ve, p.ve.n))

    r1_row = al.sb("r1_row", [1, TW], F32)
    xn_sb = al.sb("xn_sb", [128, 8 * TW], BF16)

    tiles = [(k, W * k, W) for k in range(4)] + [(4, TW - W, W)]
    for tk, c0, wd in tiles:
        p.wait("sync", p.ve)
        p.wait("sync", p.sc)
        p.wait("sync", p.mm)
        for ch in range(8):
            p.dma(x_sb[:, ch * W:ch * W + wd],
                  g["x_fm"][128 * ch:128 * (ch + 1), c0:c0 + wd])
        ln_stats(x_sb, wd, LN_EPS)
        R = row_sb
        mu, m2, var, t1, r1, srt = (R[:, i * W:i * W + wd] for i in range(6))
        p.vop("tensor_mul", m2, var, r1, inc=False)
        p.vop("tensor_scalar_add", m2, m2, LN_EPS, inc=False)
        p.vop("reciprocal", m2, m2, inc=False)
        p.vop("tensor_mul", m2, m2, r1)
        p.wait("scalar", p.ve)
        p.sop("activation", m2, m2, AF.Sqrt)
        p.sop("activation", r1_row[:, c0:c0 + wd], r1, AF.Sqrt)
        guard(5)
        p.wait("tensor", p.sc)
        p.mmul(pp[5][:, 0:wd], onesr[:, :], m2, True, True)
        p.wait("vector", p.mm)
        p.vop("tensor_copy", rho_rep[:, 0:wd], pp[5][:, 0:wd])
        ppg[5].append((p.ve, p.ve.n))
        lo = max(c0, HALO)
        for ch in range(8):
            p.vop("tensor_sub", t32a[:, 0:wd], x_sb[:, ch * W:ch * W + wd],
                  mu_rep[:, 0:wd], inc=False)
            if c0 + wd > HALO:
                p.wait("vector", p.ld)
                p.vop("tensor_copy", t32b[:, 0:wd], t32a[:, 0:wd])
                p.wait("sync", p.ve)
                p.dma(g["xmu_sp"][128 * ch:128 * (ch + 1), lo - HALO:c0 + wd - HALO],
                      t32b[:, lo - c0:wd])
            p.vop("tensor_mul", t32a[:, 0:wd], t32a[:, 0:wd], rho_rep[:, 0:wd], inc=False)
            p.vop("tensor_scalar", xn_sb[:, ch * TW + c0:ch * TW + c0 + wd],
                  t32a[:, 0:wd], vec["mg"][:, ch:ch + 1], vec["mb"][:, ch:ch + 1],
                  OP.mult, OP.add, inc=True)
    if "t_xn" in g:
        p.wait("sync", p.ve)
        for ch in range(8):
            p.dma(g["t_xn"][128 * ch:(ch + 1) * 128, :], xn_sb[:, ch * TW:(ch + 1) * TW])
    if "t_rows" in g:
        p.wait("sync", p.ve); p.wait("sync", p.sc)
        p.dma(g["t_rows"][:, :], row_sb[0:1, :])
    if "t_rhot" in g:
        p.wait("sync", p.ve)
        p.dma(g["t_rhot"][:, :], rho_rep[:, 0:16])
    if "t_xmu" in g:
        p.wait("sync", p.ve)
        p.dma(g["t_xmu"][:, :], g["xmu_sp"][:, :])
    if "t_r1" in g:
        p.wait("sync", p.sc)
        p.dma(g["t_r1"][:, :], r1_row[0:1, HALO:TW])

    wip_sb = al.sb("wip_sb", [128, 8 * 2 * DI], BF16)
    for c8 in range(8):
        p.dma(wip_sb[:, c8 * 2 * DI:(c8 + 1) * 2 * DI],
              g["wip_t"][128 * c8:128 * (c8 + 1), :], eng="gpsimd")
    wxp_sb = al.sb("wxp_sb", [128, 16 * 96], BF16)
    for c16 in range(16):
        p.dma(wxp_sb[:, c16 * 96:(c16 + 1) * 96],
              g["wxp_t"][128 * c16:128 * (c16 + 1), :], eng="gpsimd")
    wdt_sb = al.sb("wdt_sb", [DTR, DI], BF16)
    p.dma(wdt_sb[:, :], g["wdt_t"][:, :], eng="gpsimd")
    gd_weights = p.gd.n

    xsp_g = al.sb("xsp_g", [128, TW], BF16)
    xs_g = al.sb("xs_g", [128, TLOC], BF16)
    cvw = al.sb("cvw", [128, W + HALO], F32)
    cv_a = al.sb("cv_a", [128, W], F32)
    cv_b = al.sb("cv_b", [128, W], F32)
    dtbc = al.sb("dtbc", [96, TLOC], BF16)

    p.wait("tensor", p.gd, gd_weights)
    p.wait("tensor", p.ve)
    for gch in range(16):
        for tk, c0, wd in tiles:
            q = tk % 2
            guard(q)
            for ch in range(8):
                p.mmul(pp[q][:, 0:wd],
                       wip_sb[:, ch * 2 * DI + gch * 128:ch * 2 * DI + gch * 128 + 128],
                       xn_sb[:, ch * TW + c0:ch * TW + c0 + wd], ch == 0, ch == 7)
            p.wait("vector", p.mm)
            p.wait("vector", p.gd)
            p.vop("tensor_copy", xsp_g[:, c0:c0 + wd], pp[q][:, 0:wd])
            ppg[q].append((p.ve, p.ve.n))
        if gch == 0 and "t_xsp" in g:
            p.wait("sync", p.ve)
            p.dma(g["t_xsp"][:, :], xsp_g[:, :])
        for kt in range(4):
            t0 = kt * W
            p.wait("vector", p.ve)
            p.wait("vector", p.sc)
            p.vop("tensor_copy", cvw[:, :], xsp_g[:, t0:t0 + W + HALO], inc=False)
            p.vop("tensor_scalar", cv_a[:, :], cvw[:, 0:W],
                  vec["cw"][:, gch * 4 + 0:gch * 4 + 1], vec["cb"][:, gch:gch + 1],
                  OP.mult, OP.add, inc=False)
            for j in (1, 2, 3):
                src, dst = (cv_a, cv_b) if j % 2 == 1 else (cv_b, cv_a)
                p.vop("scalar_tensor_tensor", dst[:, :], cvw[:, j:W + j],
                      vec["cw"][:, gch * 4 + j:gch * 4 + j + 1], src[:, :],
                      OP.mult, OP.add, inc=(j == 3))
            if gch == 0 and kt == 0 and "t_cv" in g:
                p.wait("sync", p.ve)
                p.dma(g["t_cv"][:, :], cv_b[:, :])
                p.dma(g["t_cvw"][:, :], cvw[:, :])
            p.wait("scalar", p.ve)
            p.wait("scalar", p.gd)
            p.wait("scalar", p.mm)
            p.wait("scalar", p.ld)
            p.sop("activation", xs_g[:, t0:t0 + W], cv_b[:, :], AF.Silu)
            if gch == 0 and kt == 0 and "t_xg" in g:
                p.wait("sync", p.sc)
                p.dma(g["t_xg"][:, :], xs_g[:, 0:W])
            qx = 2 + kt
            if gch == 0:
                guard(qx)
            p.wait("tensor", p.sc)
            p.mmul(pp[qx][0:96, 0:W], wxp_sb[:, gch * 96:(gch + 1) * 96],
                   xs_g[:, t0:t0 + W], gch == 0, gch == 15, inc_force=True)
        p.wait("sync", p.sc)
        for lt in range(2):
            p.dma(g["snd_xs"][lt * DI + gch * 128:lt * DI + (gch + 1) * 128, :],
                  xs_g[:, lt * TCW:(lt + 1) * TCW])
    p.wait("scalar", p.mm)
    for kt in range(4):
        p.sop("activation", dtbc[:, kt * W:(kt + 1) * W], pp[2 + kt][0:96, 0:W], AF.Copy)
        ppg[2 + kt].append((p.sc, p.sc.n))
    p.wait("sync", p.sc)
    for lt in range(2):
        p.dma(g["bc_snd"][lt * 32:(lt + 1) * 32, :], dtbc[64:96, lt * TCW:(lt + 1) * TCW])
    if "t_bc" in g:
        p.wait("sync", p.ld)
        for lt in range(2):
            p.dma(g["t_bc"][:, lt * TCW:(lt + 1) * TCW], g["bc_snd"][lt * 32:(lt + 1) * 32, :])

    p.wait("gpsimd", p.gd)
    p.wait("gpsimd", p.ld)
    for nm_i, nm_o in (("snd_xs", "ag_xs"), ("bc_snd", "ag_bc")):
        p.emit("gpsimd", "collective_compute", "AllGather", OP.bypass,
               replica_groups=GROUPS, ins=[g[nm_i][:, :]], outs=[g[nm_o][:, :]],
               inc=(p.cc, 1))

    zbf = al.sb("zbf", [128, TLOC], BF16)
    for gch in range(16):
        eb = 16 + gch
        p.wait("scalar", p.ld)
        for tk, c0, wd in tiles:
            if c0 + wd <= HALO:
                continue
            q = tk % 2
            guard(q)
            for ch in range(8):
                p.mmul(pp[q][:, 0:wd],
                       wip_sb[:, ch * 2 * DI + eb * 128:ch * 2 * DI + eb * 128 + 128],
                       xn_sb[:, ch * TW + c0:ch * TW + c0 + wd], ch == 0, ch == 7)
            p.wait("scalar", p.mm)
            zlo = max(c0, HALO)
            p.sop("activation", zbf[:, zlo - HALO:c0 + wd - HALO],
                  pp[q][:, zlo - c0:wd], AF.Silu)
            ppg[q].append((p.sc, p.sc.n))
        p.wait("sync", p.sc)
        for lt in range(2):
            p.dma(g["snd_z"][lt * DI + gch * 128:lt * DI + (gch + 1) * 128, :],
                  zbf[:, lt * TCW:(lt + 1) * TCW])
    if "t_zs" in g:
        p.wait("sync", p.ld)
        for lt in range(2):
            p.dma(g["t_zs"][:, lt * TCW:(lt + 1) * TCW], g["snd_z"][lt * DI:(lt + 1) * DI, :])
    if "t_xs" in g:
        p.wait("sync", p.ld)
        for lt in range(2):
            p.dma(g["t_xs"][:, lt * TCW:(lt + 1) * TCW], g["snd_xs"][lt * DI:(lt + 1) * DI, :])

    p.wait("gpsimd", p.ld)
    p.emit("gpsimd", "collective_compute", "AllGather", OP.bypass,
           replica_groups=GROUPS, ins=[g["snd_z"][:, :]], outs=[g["ag_z"][:, :]],
           inc=(p.cc, 1))

    sp_t1 = al.sb("sp_t1", [128, W], F32)
    sp_t2 = al.sb("sp_t2", [128, W], BF16)
    dl_bf = al.sb("dl_bf", [128, TLOC], BF16)
    p.wait("tensor", p.sc)
    for dch in range(16):
        p.wait("vector", p.ld)
        for kt in range(4):
            t0 = kt * W
            q = 6
            guard(q)
            p.mmul(pp[q][:, 0:W], wdt_sb[:, dch * 128:(dch + 1) * 128],
                   dtbc[0:64, t0:t0 + W], True, True)
            p.wait("scalar", p.mm)
            p.wait("scalar", p.ve)
            p.sop("activation", sp_t1[:, :], pp[q][:, 0:W], AF.Abs,
                  bias=vec["dtb"][:, dch:dch + 1])
            p.sop("activation", sp_t1[:, :], sp_t1[:, :], AF.Exp, scale=-1.0)
            p.sop("activation", sp_t1[:, :], sp_t1[:, :], AF.Ln, bias=1.0)
            p.sop("activation", sp_t2[:, :], pp[q][:, 0:W], AF.Relu,
                  bias=vec["dtb"][:, dch:dch + 1])
            ppg[q].append((p.sc, p.sc.n))
            p.wait("vector", p.sc)
            p.vop("tensor_add", dl_bf[:, t0:t0 + W], sp_t2[:, :], sp_t1[:, :])
        p.wait("sync", p.ve)
        for lt in range(2):
            p.dma(g["snd_d"][lt * DI + dch * 128:lt * DI + (dch + 1) * 128, :],
                  dl_bf[:, lt * TCW:(lt + 1) * TCW])
    if "t_delta" in g:
        p.wait("sync", p.ld)
        for lt in range(2):
            p.dma(g["t_delta"][:, lt * TCW:(lt + 1) * TCW], g["snd_d"][lt * DI:(lt + 1) * DI, :])

    p.barrier()
    al.free("wip_sb", "wxp_sb", "wdt_sb", "xn_sb", "xsp_g", "xs_g", "cvw",
            "cv_a", "cv_b", "zbf", "sp_t1", "sp_t2", "dl_bf", "dtbc")

    p.wait("gpsimd", p.gd)
    p.wait("gpsimd", p.ld)
    p.emit("gpsimd", "collective_compute", "AllGather", OP.bypass,
           replica_groups=GROUPS, ins=[g["snd_d"][:, :]], outs=[g["ag_d"][:, :]],
           inc=(p.cc, 1))
    own_done = p.gd.n

    wo_sb = al.sb("wo_sb", [128, 8 * D], BF16)
    for ch in range(8):
        p.dma(wo_sb[:, ch * D:(ch + 1) * D],
              g["wout_t_half"][128 * ch:128 * (ch + 1), :], eng="gpsimd")
    gd_wo = p.gd.n

    dl_sb = al.sb("dl_sb", [128, 8 * TCW], BF16)
    xs_sb = al.sb("xs_sb", [128, 8 * TCW], BF16)
    dx_sb = al.sb("dx_sb", [128, 8 * TCW], BF16)
    zrow = al.sb("zrow", [128, 2 * TCW], BF16)
    brep = al.sb("brep", [128, 2 * TCW], BF16)
    crep = al.sb("crep", [128, 2 * TCW], BF16)
    dA_sb = al.sb("dA_sb", [128, 2 * TCW], F32)
    u_sb = al.sb("u_sb", [128, TCW], BF16)
    h_sb = al.sb("h_sb", [128, 2 * TCW], BF16)
    gt_sb = al.sb("gt_sb", [128, TCW], BF16)
    y_acc = al.sb("y_acc", [128, 8 * TCW], F32)
    yg_sb = al.sb("yg_sb", [128, 8 * TCW], BF16)
    hst = al.sb("hst", [128, 8 * DS], F32)
    p.vop("memset", hst[:, :], 0.0)

    ve_after_s = {}
    ve_after_i = {}
    bc_ready = {}
    dA_ready = {}

    def emit_bc(tc_, s_):
        gs_ = tc_ * DS + s_
        sl_ = (gs_ % 2) * TCW
        if gs_ - 2 in ve_after_s:
            p.wait("gpsimd", p.ve, ve_after_s[gs_ - 2])
        p.dma(brep[:, sl_:sl_ + TCW], g["ag_bc"][tc_ * 32 + s_:tc_ * 32 + s_ + 1, :]
              .broadcast_to([128, TCW]), eng="gpsimd")
        p.dma(crep[:, sl_:sl_ + TCW], g["ag_bc"][tc_ * 32 + 16 + s_:tc_ * 32 + 17 + s_, :]
              .broadcast_to([128, TCW]), eng="gpsimd")
        bc_ready[gs_] = p.gd.n

    def emit_dA(tc_, s_, dch_):
        gi_ = tc_ * 128 + s_ * 8 + dch_
        sl_ = (gi_ % 2) * TCW
        if gi_ - 2 in ve_after_i:
            p.wait("scalar", p.ve, ve_after_i[gi_ - 2])
        p.sop("activation", dA_sb[:, sl_:sl_ + TCW],
              dl_sb[:, dch_ * TCW:(dch_ + 1) * TCW], AF.Exp,
              scale=vec["A_sc"][:, dch_ * 16 + s_:dch_ * 16 + s_ + 1])
        dA_ready[gi_] = p.sc.n

    for tc in range(4):
        blk = tc // 2
        lt = (tc % 2) * TCW
        p.wait("gpsimd", p.cc)
        p.wait("gpsimd", p.ve)
        for dch in range(8):
            p.emit("gpsimd", "indirect_dma_start", inc=(p.gd, 16),
                   out=dl_sb[:, dch * TCW:(dch + 1) * TCW], out_offset=None,
                   in_=g["ag_d"][:, :],
                   in_offset=IndirectOffsetOnAxis(ap=IDX_SCAN(tc, dch), axis=0))
            p.emit("gpsimd", "indirect_dma_start", inc=(p.gd, 16),
                   out=xs_sb[:, dch * TCW:(dch + 1) * TCW], out_offset=None,
                   in_=g["ag_xs"][:, :],
                   in_offset=IndirectOffsetOnAxis(ap=IDX_SCAN(tc, dch), axis=0))
        ld_tiles = p.gd.n
        p.wait("vector", p.gd, ld_tiles)
        for dch in range(8):
            sl = slice(dch * TCW, (dch + 1) * TCW)
            p.vop("tensor_mul", dx_sb[:, sl], dl_sb[:, sl], xs_sb[:, sl], inc=False)
            p.vop("memset", y_acc[:, sl], 0.0, inc=(dch == 7))
        emit_bc(tc, 0)
        p.wait("scalar", p.gd, ld_tiles)
        emit_dA(tc, 0, 0)
        for s in range(DS):
            gs = tc * DS + s
            bsl = (gs % 2) * TCW
            for dch in range(8):
                gi = tc * 128 + s * 8 + dch
                if dch == 7 and s < DS - 1:
                    emit_bc(tc, s + 1)
                if s * 8 + dch < 127:
                    emit_dA(tc, s + (dch + 1) // 8, (dch + 1) % 8)
                asl = (gi % 2) * TCW
                sl = slice(dch * TCW, (dch + 1) * TCW)
                if dch == 0:
                    p.wait("vector", p.gd, bc_ready[gs])
                p.wait("vector", p.sc, dA_ready[gi])
                p.vop("tensor_mul", u_sb[:, :], dx_sb[:, sl],
                      brep[:, bsl:bsl + TCW], inc=False)
                p.vop("tensor_tensor_scan", h_sb[:, asl:asl + TCW],
                      dA_sb[:, asl:asl + TCW],
                      u_sb[:, :], hst[:, dch * DS + s:dch * DS + s + 1],
                      OP.mult, OP.add, inc=False)
                p.vop("tensor_mul", gt_sb[:, :], h_sb[:, asl:asl + TCW],
                      crep[:, bsl:bsl + TCW], inc=False)
                p.vop("tensor_add", y_acc[:, sl], y_acc[:, sl], gt_sb[:, :])
                ve_after_i[gi] = p.ve.n
                if dch == 7:
                    ve_after_s[gs] = p.ve.n
                p.wait("scalar", p.ve, ve_after_i[gi])
                p.sop("copy", hst[:, dch * DS + s:dch * DS + s + 1],
                      h_sb[:, asl + TCW - 1:asl + TCW])
        p.wait("vector", p.mm)
        ve_gate = {}
        zr_ready = {}

        def emit_zrow(dch_):
            if dch_ - 2 in ve_gate:
                p.wait("gpsimd", p.ve, ve_gate[dch_ - 2])
            else:
                p.wait("gpsimd", p.ve)
            p.emit("gpsimd", "indirect_dma_start", inc=(p.gd, 16),
                   out=zrow[:, (dch_ % 2) * TCW:(dch_ % 2) * TCW + TCW],
                   out_offset=None, in_=g["ag_z"][:, :],
                   in_offset=IndirectOffsetOnAxis(ap=IDX_SCAN(tc, dch_), axis=0))
            zr_ready[dch_] = p.gd.n

        emit_zrow(0)
        for dch in range(8):
            sl = slice(dch * TCW, (dch + 1) * TCW)
            if dch < 7:
                emit_zrow(dch + 1)
            p.wait("vector", p.gd, zr_ready[dch])
            p.vop("tensor_scalar_mul", gt_sb[:, :], xs_sb[:, sl],
                  vec["Dp_sc"][:, dch:dch + 1], inc=False)
            p.vop("tensor_add", y_acc[:, sl], y_acc[:, sl], gt_sb[:, :], inc=False)
            p.vop("tensor_mul", yg_sb[:, sl], zrow[:, (dch % 2) * TCW:(dch % 2) * TCW + TCW],
                  y_acc[:, sl])
            ve_gate[dch] = p.ve.n
        if "t_y" in g:
            p.wait("sync", p.ve)
            for dch in range(8):
                p.dma(g["t_y"][dch * 128:(dch + 1) * 128, tc * TCW // 2:(tc + 1) * TCW // 2],
                      y_acc[:, dch * TCW:dch * TCW + TCW // 2])
        rsbuf = g["rs_in_a"] if tc % 2 == 0 else g["rs_in_b"]
        p.wait("tensor", p.ve)
        p.wait("tensor", p.gd, gd_wo)
        for dob in range(8):
            for sub in range(2):
                q = (2 * dob + sub) % 2
                guard(q)
                for dch in range(8):
                    p.mmul(pp[q][:, 0:W],
                           wo_sb[:, dch * D + dob * 128:dch * D + dob * 128 + 128],
                           yg_sb[:, dch * TCW + sub * W:dch * TCW + (sub + 1) * W],
                           dch == 0, dch == 7)
                p.wait("vector", p.mm)
                p.wait("vector", p.ld)
                sl_ev = slice(((2 * dob + sub) % 4) * W, ((2 * dob + sub) % 4) * W + W)
                p.vop("tensor_copy", x2_sb[:, sl_ev], pp[q][:, 0:W])
                ppg[q].append((p.ve, p.ve.n))
                p.wait("sync", p.ve)
                p.dma(rsbuf[blk * D + dob * 128:blk * D + (dob + 1) * 128,
                            sub * W:(sub + 1) * W], x2_sb[:, sl_ev])
        if tc == 2:
            p.wait("gpsimd", p.ld)
            p.emit("gpsimd", "collective_compute", "ReduceScatter", OP.add,
                   replica_groups=GROUPS, ins=[g["rs_in_a"][:, :]],
                   outs=[g["rs_out_a"][:, :]], inc=(p.cc, 1))
    p.wait("gpsimd", p.ld)
    p.emit("gpsimd", "collective_compute", "ReduceScatter", OP.add,
           replica_groups=GROUPS, ins=[g["rs_in_b"][:, :]],
           outs=[g["rs_out_b"][:, :]], inc=(p.cc, 1))
    rs_done = p.cc.n
    p.barrier()
    al.free("dl_sb", "xs_sb", "dx_sb", "zrow", "brep", "crep", "dA_sb", "u_sb",
            "h_sb", "gt_sb", "y_acc", "yg_sb", "hst", "wo_sb")

    p.wait("sync", p.sc)
    p.dma(g["rowb"][0:1, 0:TLOC], r1_row[0:1, HALO:TW])
    r1_ld = p.ld.n
    r1rep = al.sb("r1rep", [128, W], F32)
    xr_sb = al.sb("xr_sb", [128, 8 * W], F32)
    p.wait("sync", p.cc, rs_done)
    for kt in range(4):
        t0 = kt * W
        p.wait("gpsimd", p.ld, r1_ld)
        p.wait("gpsimd", p.ve)
        p.dma(r1rep[:, :], g["rowb"][0:1, t0:t0 + W].broadcast_to([128, W]), eng="gpsimd")
        rso = g["rs_out_a"] if (t0 // TCW) == 0 else g["rs_out_b"]
        ct = t0 % TCW
        p.wait("sync", p.ve)
        for ch in range(8):
            p.dma(x_sb[:, ch * W:(ch + 1) * W],
                  g["x_fm"][128 * ch:128 * (ch + 1), t0 + HALO:t0 + HALO + W])
            p.dma(xr_sb[:, ch * W:(ch + 1) * W],
                  g["xmu_sp"][128 * ch:128 * (ch + 1), t0:t0 + W])
        p.wait("vector", p.ld)
        p.wait("vector", p.gd)
        for ch in range(8):
            sl = slice(ch * W, (ch + 1) * W)
            p.vop("tensor_mul", x2_sb[:, sl], xr_sb[:, sl], r1rep[:, :], inc=False)
            p.vop("tensor_add", x2_sb[:, sl], x2_sb[:, sl], x_sb[:, sl], inc=(ch == 7))
        p.wait("sync", p.ve)
        for ch in range(8):
            p.dma(xr_sb[:, ch * W:(ch + 1) * W], rso[ch * 128:(ch + 1) * 128, ct:ct + W])
        p.wait("vector", p.ld)
        for ch in range(8):
            sl = slice(ch * W, (ch + 1) * W)
            p.vop("tensor_add", x2_sb[:, sl], x2_sb[:, sl], xr_sb[:, sl], inc=(ch == 7))
        p.wait("sync", p.ve)
        for ch in range(8):
            p.dma(g["x2_sp"][128 * ch:(ch + 1) * 128, t0:t0 + W], x2_sb[:, ch * W:(ch + 1) * W])
        ln_stats(x2_sb, W, LN_EPS)
        for ch in range(8):
            sl = slice(ch * W, (ch + 1) * W)
            p.vop("tensor_sub", xr_sb[:, sl], x2_sb[:, sl], mu_rep[:, 0:W], inc=False)
            p.vop("tensor_mul", xr_sb[:, sl], xr_sb[:, sl], rho_rep[:, 0:W], inc=(ch == 7))
        p.wait("sync", p.ve)
        for ch in range(8):
            p.dma(g["xf_sp"][128 * ch:(ch + 1) * 128, t0:t0 + W], xr_sb[:, ch * W:(ch + 1) * W])
    if "t_x2" in g:
        p.wait("sync", p.ld)
        p.dma(g["t_x2"][:, :], g["x2_sp"][:, :])

    if phases in ("attn", "full"):
        p.barrier()
        al.free("xr_sb")
        build_attn(nc, p, g, al, vec, pp, ppg, guard, ln_stats, locals())
    if phases == "full":
        p.barrier()
        build_moe(nc, p, g, al, vec, pp, ppg, guard, ln_stats, locals())

    p.wait("sync", p.ld)
    p.wait("sync", p.gd)
    p.wait("sync", p.st)

    with nc.Block() as block:
        @block.sync
        def _(e):
            for fn in p.streams["sync"]:
                fn(e)

        @block.gpsimd
        def _(e):
            for fn in p.streams["gpsimd"]:
                fn(e)

        @block.scalar
        def _(e):
            for fn in p.streams["scalar"]:
                fn(e)

        @block.vector
        def _(e):
            for fn in p.streams["vector"]:
                fn(e)

        @block.tensor
        def _(e):
            for fn in p.streams["tensor"]:
                fn(e)

    return nc


def build_attn(nc, p, g, al, vec, pp, ppg, guard, ln_stats, env):
    x_sb = env["x_sb"]; sq_sb = env["sq_sb"]; row_sb = env["row_sb"]
    t32a = env["t32a"]; ones32 = env["ones32"]; onesr = env["onesr"]
    onesbf = env["onesbf"]
    idn_sb = env["idn_sb"]; idn32_sb = env["idn32_sb"]
    rho_rep = env["rho_rep"]; x2_sb = env["x2_sb"]

    ao_sb = al.sb("ao_sb", [128, 8 * TLOC], BF16)
    q_sb = al.sb("q_sb", [128, 8 * TLOC], BF16)
    kc_sb = al.sb("kc_sb", [128, 2 * TLOC], BF16)
    vc_sb = al.sb("vc_sb", [128, 2 * TLOC], BF16)
    vt_bf = al.sb("vt_bf", [128, 256], BF16)
    wq_sb = al.sb("wq_sb", [128, 8 * D], BF16)
    for ch in range(8):
        p.dma(wq_sb[:, ch * D:(ch + 1) * D], g["wq_t"][128 * ch:(ch + 1) * 128, :],
              eng="gpsimd")
    wkv_sb = al.sb("wkv_sb", [128, 8 * 512], BF16)
    for ch in range(8):
        p.dma(wkv_sb[:, ch * 512:ch * 512 + 256], g["wkc_t"][128 * ch:(ch + 1) * 128, :],
              eng="gpsimd")
        p.dma(wkv_sb[:, ch * 512 + 256:(ch + 1) * 512],
              g["wvc_t"][128 * ch:(ch + 1) * 128, :], eng="gpsimd")
    gd_w = p.gd.n

    nx_sb = al.sb("nx_sb", [128, 8 * TLOC], BF16)
    for kt in range(4):
        t0 = kt * W
        p.wait("sync", p.ve); p.wait("sync", p.sc); p.wait("sync", p.mm)
        for ch in range(8):
            p.dma(x_sb[:, ch * W:(ch + 1) * W], g["xf_sp"][128 * ch:(ch + 1) * 128, t0:t0 + W])
        ln_stats(x_sb, W, RMS_EPS, rms_only=True)
        for ch in range(8):
            p.vop("tensor_mul", t32a[:, 0:W], x_sb[:, ch * W:(ch + 1) * W],
                  rho_rep[:, 0:W], inc=False)
            p.vop("tensor_scalar", nx_sb[:, ch * TLOC + t0:ch * TLOC + t0 + W],
                  t32a[:, 0:W], vec["g1v"][:, ch:ch + 1], None, OP.mult, inc=(ch == 7))
    if "t_nx1" in g:
        p.wait("sync", p.ve)
        for ch in range(8):
            p.dma(g["t_nx1"][128 * ch:(ch + 1) * 128, :], nx_sb[:, ch * TLOC:(ch + 1) * TLOC])

    p.wait("tensor", p.gd, gd_w)
    for kt in range(4):
        t0 = kt * W
        for qb in range(8):
            q = qb % 2
            guard(q)
            for ch in range(8):
                p.mmul(pp[q][:, 0:W], wq_sb[:, ch * D + qb * 128:ch * D + qb * 128 + 128],
                       nx_sb[:, ch * TLOC + t0:ch * TLOC + t0 + W], ch == 0, ch == 7)
            p.wait("scalar", p.mm)
            p.sop("activation", q_sb[:, qb * TLOC + t0:qb * TLOC + t0 + W],
                  pp[q][:, 0:W], AF.Copy)
            ppg[q].append((p.sc, p.sc.n))
        for kb in range(2):
            for which, dst in ((0, kc_sb), (256, vc_sb)):
                q = 2 + kb + (2 if which else 0)
                guard(q)
                for ch in range(8):
                    p.mmul(pp[q][:, 0:W],
                           wkv_sb[:, ch * 512 + which + kb * 128:ch * 512 + which + kb * 128 + 128],
                           nx_sb[:, ch * TLOC + t0:ch * TLOC + t0 + W], ch == 0, ch == 7)
                p.wait("scalar", p.mm)
                p.sop("activation", dst[:, kb * TLOC + t0:kb * TLOC + t0 + W],
                      pp[q][:, 0:W], AF.Copy)
                ppg[q].append((p.sc, p.sc.n))
    p.barrier()
    al.free("nx_sb")
    p.wait("sync", p.sc)
    for kb in range(2):
        p.dma(g["kc_snd"][kb * 128:(kb + 1) * 128, :], kc_sb[:, kb * TLOC:(kb + 1) * TLOC])
    for tt in range(16):
        for kb in range(2):
            q = 6
            guard(q)
            p.wait("vector", p.sc)
            p.wait("vector", p.mm)
            p.vop("tensor_copy", t32a[:, 0:128],
                  vc_sb[:, kb * TLOC + tt * 128:kb * TLOC + (tt + 1) * 128])
            p.wait("tensor", p.ve)
            p.emit("tensor", "transpose", pp[q][:, 0:128], t32a[:, 0:128],
                   idn32_sb[:, :], inc=(p.mm, 1))
            p.wait("scalar", p.mm)
            p.wait("scalar", p.ld)
            p.sop("activation", vt_bf[:, kb * 128:(kb + 1) * 128], pp[q][:, 0:128], AF.Copy)
            ppg[q].append((p.sc, p.sc.n))
        p.wait("sync", p.sc)
        p.dma(g["vct_snd"][tt * 128:(tt + 1) * 128, :], vt_bf[:, :])
    p.wait("gpsimd", p.ld)
    for nm_i, nm_o in (("kc_snd", "ag_kc"), ("vct_snd", "ag_vc")):
        p.emit("gpsimd", "collective_compute", "AllGather", OP.bypass,
               replica_groups=GROUPS, ins=[g[nm_i][:, :]], outs=[g[nm_o][:, :]],
               inc=(p.cc, 1))
    kv_cc = p.cc.n
    p.barrier()
    al.free("wq_sb", "wkv_sb", "vt_bf", "vc_sb", "kc_sb")

    if "t_vc" in g:
        p.wait("sync", p.cc)
        p.dma(g["t_vc"][:, :], g["ag_vc"][0:256, :])
    kcg_sb = al.sb("kcg_sb", [128, 2 * 4096], BF16)
    vcg_sb = al.sb("vcg_sb", [128, 32 * 256], BF16)
    p.wait("sync", p.cc, kv_cc)
    for kb in range(2):
        for half in range(2):
            p.dma(kcg_sb[:, kb * 4096 + half * TLOC:kb * 4096 + (half + 1) * TLOC],
                  g["ag_kc"][half * 256 + kb * 128:half * 256 + (kb + 1) * 128, :])
    for tb in range(32):
        p.dma(vcg_sb[:, tb * 256:(tb + 1) * 256], g["ag_vc"][tb * 128:(tb + 1) * 128, :])
    kv_ld = p.ld.n

    kn_row = al.sb("kn_row", [1, 16], F32)
    p.wait("scalar", p.ld, kv_ld)
    for seg in range(8):
        guard(7)
        for kb in range(2):
            sq = sq_sb[:, kb * W:(kb + 1) * W]
            p.wait("scalar", p.mm)
            p.sop("activation", sq,
                  kcg_sb[:, kb * 4096 + seg * W:kb * 4096 + (seg + 1) * W], AF.Square)
            p.wait("tensor", p.sc)
            p.mmul(pp[7][0:1, 0:W], ones32[:, :], sq, kb == 0, kb == 1, inc_force=True)
        p.wait("vector", p.mm)
        p.vop("reduce_max", kn_row[:, seg:seg + 1], pp[7][0:1, 0:W], AX)
        ppg[7].append((p.ve, p.ve.n))
    p.vop("reduce_max", kn_row[:, 8:9], kn_row[:, 0:8], AX)
    p.wait("vector", p.ve)

    bq_row = al.sb("bq_row", [1, TLOC], F32)
    brep_b = al.sb("brep_b", [128, TLOC], BF16)
    sc_f = al.sb("sc_f", [128, W], F32)
    prob_bf = al.sb("prob_bf", [128, 2 * W], BF16)
    sum_row = al.sb("sum_row", [1, TLOC], F32)
    rec_rep = al.sb("rec_rep", [128, TLOC], BF16)

    p.wait("tensor", p.ld, kv_ld)
    p.wait("tensor", p.sc)
    for hd in range(4):
        for seg in range(4):
            guard(7)
            p.wait("scalar", p.ve)
            for kb in range(2):
                sq = sq_sb[:, kb * W:(kb + 1) * W]
                p.wait("scalar", p.mm)
                p.sop("activation", sq,
                      q_sb[:, (2 * hd + kb) * TLOC + seg * W:(2 * hd + kb) * TLOC + (seg + 1) * W],
                      AF.Square)
                p.wait("tensor", p.sc)
                p.mmul(pp[7][0:1, 0:W], ones32[:, :], sq, kb == 0, kb == 1, inc_force=True)
            p.wait("vector", p.mm)
            p.vop("tensor_copy", bq_row[:, seg * W:(seg + 1) * W], pp[7][0:1, 0:W])
            ppg[7].append((p.ve, p.ve.n))
        p.vop("tensor_scalar", bq_row[:, :], bq_row[:, :], kn_row[0:1, 8:9], None,
              OP.mult)
        p.wait("scalar", p.ve)
        p.sop("activation", bq_row[:, :], bq_row[:, :], AF.Sqrt)
        p.wait("vector", p.sc)
        p.vop("tensor_scalar_mul", bq_row[:, :], bq_row[:, :], 1.0 / 16.0)
        p.wait("gpsimd", p.ve)
        p.emit("gpsimd", "dma_start", inc=(p.gd, 16), out=g["rowbf"][0:1, 0:TLOC],
               in_=bq_row[0:1, :])
        p.wait("gpsimd", p.gd)
        p.dma(brep_b[:, :], g["rowbf"][0:1, 0:TLOC].broadcast_to([128, TLOC]),
              eng="gpsimd")
        bq_n = p.gd.n

        for tq in range(4):
            t0 = tq * W
            guard(2); guard(3)
            guard(6)
            for tk in range(32):
                q = tk % 2
                guard(q)
                for kb in range(2):
                    p.mmul(pp[q][:, 0:W],
                           kcg_sb[:, kb * 4096 + tk * 128:kb * 4096 + (tk + 1) * 128],
                           q_sb[:, (2 * hd + kb) * TLOC + t0:(2 * hd + kb) * TLOC + t0 + W],
                           kb == 0, kb == 1)
                p.wait("vector", p.mm)
                p.wait("vector", p.gd, bq_n)
                p.wait("vector", p.sc)
                p.vop("scalar_tensor_tensor", sc_f[:, 0:W], pp[q][:, 0:W], 1.0 / 16.0,
                      brep_b[:, t0:t0 + W], OP.mult, OP.subtract)
                ppg[q].append((p.ve, p.ve.n))
                if hd == 0 and tq == 0 and tk == 0 and "t_attf" in g:
                    p.wait("sync", p.ve)
                    p.dma(g["t_attf"][:, 0:W], sc_f[:, 0:W])
                    p.dma(g["t_attf"][0:1, W:W + 1024], bq_row[0:1, 0:1024])
                    p.dma(g["t_attb"][:, 0:W], brep_b[:, 0:W])
                    p.wait("vector", p.ld)
                    p.wait("scalar", p.ld)
                p.wait("scalar", p.ve)
                pslot = prob_bf[:, (tk % 2) * W:(tk % 2) * W + W]
                p.wait("scalar", p.mm)
                p.sop("activation", pslot, sc_f[:, 0:W], AF.Exp)
                if hd == 0 and tq == 0 and tk == 0 and "t_attb" in g:
                    p.wait("sync", p.sc)
                    p.dma(g["t_attb"][:, W:2 * W], pslot)
                    p.wait("tensor", p.ld)
                    p.wait("scalar", p.ld)
                p.wait("tensor", p.sc)
                p.mmul(pp[6][:, 0:W], onesbf[:, :], pslot, tk == 0, tk == 31,
                       inc_last=False)
                for vb in range(2):
                    p.mmul(pp[2 + vb][:, 0:W],
                           vcg_sb[:, tk * 256 + vb * 128:tk * 256 + (vb + 1) * 128],
                           pslot, tk == 0, tk == 31, inc_last=(vb == 1))
            p.wait("vector", p.mm)
            p.vop("tensor_copy", sum_row[:, t0:t0 + W], pp[6][0:1, 0:W], inc=True)
            if hd == 0 and tq == 0 and "t_attf" in g:
                p.wait("sync", p.ve)
                p.emit("sync", "dma_start", inc=(p.ld, 16),
                       out=g["t_attf"][1:2, W:W + W], in_=sum_row[0:1, 0:W])
                p.vop("tensor_copy", t32a[:, 0:W], pp[2][:, 0:W])
                p.wait("sync", p.ve)
                p.emit("sync", "dma_start", inc=(p.ld, 16),
                       out=g["t_attf"][:, 2 * W:3 * W], in_=t32a[:, 0:W])
                p.wait("vector", p.ld)
            p.vop("reciprocal", sum_row[:, t0:t0 + W], sum_row[:, t0:t0 + W])
            ppg[6].append((p.ve, p.ve.n))
            p.wait("gpsimd", p.ve)
            p.emit("gpsimd", "dma_start", inc=(p.gd, 16),
                   out=g["rowbf"][0:1, TLOC + t0:TLOC + t0 + W], in_=sum_row[0:1, t0:t0 + W])
            p.wait("gpsimd", p.gd)
            p.dma(rec_rep[:, t0:t0 + W],
                  g["rowbf"][0:1, TLOC + t0:TLOC + t0 + W].broadcast_to([128, W]),
                  eng="gpsimd")
            p.wait("vector", p.gd)
            for vb in range(2):
                p.vop("tensor_mul",
                      ao_sb[:, (2 * hd + vb) * TLOC + t0:(2 * hd + vb) * TLOC + t0 + W],
                      rec_rep[:, t0:t0 + W], pp[2 + vb][:, 0:W])
                ppg[2 + vb].append((p.ve, p.ve.n))
    p.barrier()
    al.free("kcg_sb", "vcg_sb", "kn_row", "bq_row", "brep_b", "sc_f", "prob_bf",
            "sum_row", "rec_rep", "q_sb")
    if "t_ao" in g:
        aot = al.sb("aot", [128, W], F32)
        for ch in range(8):
            for kt in range(4):
                p.wait("vector", p.ld)
                p.vop("tensor_copy", aot[:, 0:W],
                      ao_sb[:, ch * TLOC + kt * W:ch * TLOC + (kt + 1) * W])
                p.wait("sync", p.ve)
                p.dma(g["t_ao"][128 * ch:(ch + 1) * 128, kt * W:(kt + 1) * W], aot[:, 0:W])
        al.free("aot")

    wo_sb2 = al.sb("wo_sb2", [128, 8 * D], BF16)
    for ch in range(8):
        p.dma(wo_sb2[:, ch * D:(ch + 1) * D], g["wo_t"][128 * ch:(ch + 1) * 128, :],
              eng="gpsimd")
    gd_wo2 = p.gd.n
    p.wait("tensor", p.gd, gd_wo2)
    p.wait("tensor", p.ve)
    for kt in range(4):
        t0 = kt * W
        p.wait("sync", p.ve)
        for ch in range(8):
            p.dma(x_sb[:, ch * W:(ch + 1) * W], g["xf_sp"][128 * ch:(ch + 1) * 128, t0:t0 + W])
        for ob in range(8):
            q = ob % 2
            guard(q)
            for ch in range(8):
                p.mmul(pp[q][:, 0:W], wo_sb2[:, ch * D + ob * 128:ch * D + ob * 128 + 128],
                       ao_sb[:, ch * TLOC + t0:ch * TLOC + t0 + W], ch == 0, ch == 7)
            p.wait("vector", p.mm)
            p.wait("vector", p.ld)
            p.vop("tensor_add", x2_sb[:, ob * W:(ob + 1) * W], pp[q][:, 0:W],
                  x_sb[:, ob * W:(ob + 1) * W])
            ppg[q].append((p.ve, p.ve.n))
            p.wait("sync", p.ve)
            p.dma(g["x3_sp"][128 * ob:(ob + 1) * 128, t0:t0 + W],
                  x2_sb[:, ob * W:(ob + 1) * W])
    if "t_x3" in g:
        p.wait("sync", p.ld)
        p.dma(g["t_x3"][:, :], g["x3_sp"][:, :])
    p.barrier()
    al.free("wo_sb2", "ao_sb")


def build_moe(nc, p, g, al, vec, pp, ppg, guard, ln_stats, env):
    x_sb = env["x_sb"]; row_sb = env["row_sb"]; t32a = env["t32a"]; t32b = env["t32b"]
    ones32 = env["ones32"]; idn32_sb = env["idn32_sb"]; rho_rep = env["rho_rep"]
    x2_sb = env["x2_sb"]
    xf2_sb = al.sb("xf2_sb", [128, 8 * TLOC], BF16)
    wr_sb = al.sb("wr_sb", [128, 8 * 8], BF16)
    for ch in range(8):
        p.dma(wr_sb[:, ch * 8:(ch + 1) * 8], g["wr_t"][128 * ch:(ch + 1) * 128, :],
              eng="gpsimd")
    gd_wr = p.gd.n

    for kt in range(4):
        t0 = kt * W
        p.wait("sync", p.ve); p.wait("sync", p.sc); p.wait("sync", p.mm)
        for ch in range(8):
            p.dma(x_sb[:, ch * W:(ch + 1) * W], g["x3_sp"][128 * ch:(ch + 1) * 128, t0:t0 + W])
        ln_stats(x_sb, W, RMS_EPS, rms_only=True)
        for ch in range(8):
            p.vop("tensor_mul", t32a[:, 0:W], x_sb[:, ch * W:(ch + 1) * W],
                  rho_rep[:, 0:W], inc=False)
            p.vop("tensor_scalar", xf2_sb[:, ch * TLOC + t0:ch * TLOC + t0 + W],
                  t32a[:, 0:W], vec["g2v"][:, ch:ch + 1], None, OP.mult, inc=(ch == 7))

    lgv = al.sb("lgv", [128, 5 * 128 + 3 * 16], F32)
    lgs = al.sb("lgs", [128, 128 + 16], F32)
    cmb = al.sb("cmb", [128, 128], F32)
    LOGa = 0; LMa = 128; RWa = 256; MKa = 384; SSa = 512
    M1a = 640; M2a = 656; RZa = 672
    ELa = 0; ZZa = 128

    p.wait("tensor", p.gd, gd_wr)
    for tt in range(16):
        t0 = tt * 128
        q = 6 if tt % 2 == 0 else 7
        guard(q)
        p.wait("tensor", p.ve)
        for ch in range(8):
            p.mmul(pp[q][0:128, 0:8], xf2_sb[:, ch * TLOC + t0:ch * TLOC + t0 + 128],
                   wr_sb[:, ch * 8:(ch + 1) * 8], ch == 0, ch == 7)
        p.wait("vector", p.mm)
        p.vop("tensor_add", lgv[:, LOGa + 8 * tt:LOGa + 8 * tt + 8],
              pp[q][0:128, 0:8], vec["ebias"][:, 0:8])
        ppg[q].append((p.ve, p.ve.n))

    def bc8(col):
        return lgv[:, col:col + 1].broadcast_to([128, 8])

    for tt in range(16):
        p.vop("reduce_max", lgv[:, M1a + tt:M1a + tt + 1],
              lgv[:, LOGa + 8 * tt:LOGa + 8 * tt + 8], AX, inc=(tt == 15))
    for tt in range(16):
        p.vop("tensor_tensor", lgv[:, LMa + 8 * tt:LMa + 8 * tt + 8],
              lgv[:, LOGa + 8 * tt:LOGa + 8 * tt + 8], bc8(M1a + tt),
              OP.subtract, inc=(tt == 15))
    p.wait("scalar", p.ve)
    for tt in range(16):
        p.sop("activation", lgs[:, ELa + 8 * tt:ELa + 8 * tt + 8],
              lgv[:, LMa + 8 * tt:LMa + 8 * tt + 8], AF.Exp,
              accum_out=lgs[:, ZZa + tt:ZZa + tt + 1], inc=(tt == 15))
    p.wait("vector", p.sc)
    p.vop("reciprocal", lgv[:, RZa:RZa + 16], lgs[:, ZZa:ZZa + 16])
    p.vop("tensor_copy", lgv[:, M2a:M2a + 16], lgv[:, RZa:RZa + 16])
    p.vop("tensor_copy", lgv[:, M2a:M2a + 16], lgv[:, RZa:RZa + 16])
    p.vop("tensor_copy", lgv[:, M2a:M2a + 16], lgv[:, RZa:RZa + 16])
    for tt in range(16):
        p.vop("tensor_tensor", lgv[:, RWa + 8 * tt:RWa + 8 * tt + 8],
              lgs[:, ELa + 8 * tt:ELa + 8 * tt + 8],
              lgv[:, RZa + tt:RZa + tt + 1].broadcast_to([128, 8]), OP.mult,
              inc=(tt == 15))
    for tt in range(16):
        p.vop("reduce_max", lgv[:, M1a + tt:M1a + tt + 1],
              lgv[:, RWa + 8 * tt:RWa + 8 * tt + 8], AX, inc=(tt == 15))
    for tt in range(16):
        p.vop("tensor_tensor", lgv[:, MKa + 8 * tt:MKa + 8 * tt + 8],
              lgv[:, RWa + 8 * tt:RWa + 8 * tt + 8], bc8(M1a + tt), OP.is_ge,
              inc=(tt == 15))
    p.vop("scalar_tensor_tensor", lgv[:, MKa:MKa + 128], lgv[:, MKa:MKa + 128],
          -1e30, lgv[:, RWa:RWa + 128], OP.mult, OP.add)
    for tt in range(16):
        p.vop("reduce_max", lgv[:, M2a + tt:M2a + tt + 1],
              lgv[:, MKa + 8 * tt:MKa + 8 * tt + 8], AX, inc=(tt == 15))
    for tt in range(16):
        p.vop("tensor_tensor", lgv[:, MKa + 8 * tt:MKa + 8 * tt + 8],
              lgv[:, RWa + 8 * tt:RWa + 8 * tt + 8], bc8(M2a + tt), OP.is_ge,
              inc=(tt == 15))
    p.vop("scalar_tensor_tensor", lgv[:, SSa:SSa + 128], lgv[:, MKa:MKa + 128],
          100.0, lgv[:, RWa:RWa + 128], OP.mult, OP.add)
    for tt in range(16):
        p.vop("reduce_max", lgv[:, M1a + tt:M1a + tt + 1],
              lgv[:, SSa + 8 * tt:SSa + 8 * tt + 8], AX, inc=(tt == 15))
    for tt in range(16):
        p.vop("tensor_tensor", lgv[:, LMa + 8 * tt:LMa + 8 * tt + 8],
              lgv[:, SSa + 8 * tt:SSa + 8 * tt + 8], bc8(M1a + tt), OP.subtract,
              inc=(tt == 15))
    p.wait("scalar", p.ve)
    for tt in range(16):
        p.sop("activation", lgs[:, ELa + 8 * tt:ELa + 8 * tt + 8],
              lgv[:, LMa + 8 * tt:LMa + 8 * tt + 8], AF.Exp,
              accum_out=lgs[:, ZZa + tt:ZZa + tt + 1], inc=(tt == 15))
    p.wait("vector", p.sc)
    p.vop("reciprocal", lgv[:, RZa:RZa + 16], lgs[:, ZZa:ZZa + 16])
    p.vop("tensor_copy", lgv[:, M2a:M2a + 16], lgv[:, RZa:RZa + 16])
    p.vop("tensor_copy", lgv[:, M2a:M2a + 16], lgv[:, RZa:RZa + 16])
    p.vop("tensor_copy", lgv[:, M2a:M2a + 16], lgv[:, RZa:RZa + 16])
    for tt in range(16):
        p.vop("tensor_tensor", cmb[:, 8 * tt:8 * tt + 8],
              lgs[:, ELa + 8 * tt:ELa + 8 * tt + 8],
              lgv[:, RZa + tt:RZa + tt + 1].broadcast_to([128, 8]), OP.mult,
              inc=(tt == 15))
    for tt in range(16):
        t0 = tt * 128
        q2 = 4
        guard(q2)
        p.wait("tensor", p.ve)
        p.emit("tensor", "transpose", pp[q2][0:8, 0:128], cmb[:, 8 * tt:8 * tt + 8],
               idn32_sb[:, :], inc=(p.mm, 1))
        p.wait("vector", p.mm)
        p.wait("vector", p.ld)
        p.vop("tensor_copy", t32a[0:8, 0:128], pp[q2][0:8, 0:128])
        ppg[q2].append((p.ve, p.ve.n))
        p.wait("sync", p.ve)
        p.dma(g["comb_fm"][0:8, t0:t0 + 128], t32a[0:8, 0:128])
    p.wait("vector", p.ld)
    p.vop("memset", t32a[0:1, 0:W], 1.0)
    p.wait("sync", p.ve)
    for kt in range(4):
        p.dma(g["comb_fm"][8:9, kt * W:(kt + 1) * W], t32a[0:1, 0:W])
    if "t_comb" in g:
        p.wait("sync", p.ld)
        p.dma(g["t_comb"][:, :], g["comb_fm"][:, :])
    p.barrier()
    al.free("cmb", "lgs", "lgv", "wr_sb")

    we1 = al.sb("we1", [128, 2 * 8 * DFF], BF16)
    we3 = al.sb("we3", [128, 8 * DFF], BF16)
    we2 = al.sb("we2", [128, 8 * D], BF16)
    h1_bf = al.sb("h1_bf", [128, W], BF16)
    hp_bf = al.sb("hp_bf", [128, 8 * W], BF16)
    crep_f = al.sb("crep_f", [128, W], F32)
    acc_sb = al.sb("acc_sb", [128, 8 * TCW], F32)

    steps = [(th, ex) for th in range(2) for ex in range(9)]
    mm_stage1 = {}
    mm_step = {}
    gd_w13 = {}
    gd_w3 = {}
    gd_w2 = {}

    def emit_w1(j):
        th_, ex_ = steps[j]
        off = (j % 2) * 8 * DFF
        if j - 2 in mm_stage1:
            p.wait("gpsimd", p.mm, mm_stage1[j - 2])
        for ch in range(8):
            src = g["ew1_t"][ex_, 128 * ch:(ch + 1) * 128, :] if ex_ < 8 \
                else g["sw1_t"][128 * ch:(ch + 1) * 128, :]
            p.dma(we1[:, off + ch * DFF:off + (ch + 1) * DFF], src, eng="gpsimd")
        gd_w13[j] = p.gd.n

    def emit_w3(j):
        th_, ex_ = steps[j]
        if j - 1 in mm_stage1:
            p.wait("gpsimd", p.mm, mm_stage1[j - 1])
        for ch in range(8):
            src = g["ew3_t"][ex_, 128 * ch:(ch + 1) * 128, :] if ex_ < 8 \
                else g["sw3_t"][128 * ch:(ch + 1) * 128, :]
            p.dma(we3[:, ch * DFF:(ch + 1) * DFF], src, eng="gpsimd")
        gd_w3[j] = p.gd.n

    def emit_w2(j):
        th_, ex_ = steps[j]
        if j - 1 in mm_step:
            p.wait("gpsimd", p.mm, mm_step[j - 1])
        for ch in range(8):
            if ex_ < 8:
                p.dma(we2[:, ch * D:(ch + 1) * D],
                      g["ew2_t"][ex_, 128 * ch:(ch + 1) * 128, :], eng="gpsimd")
            else:
                p.dma(we2[:, ch * D:(ch + 1) * D],
                      g["sw2_t"][128 * ch:(ch + 1) * 128, :], eng="gpsimd")
        gd_w2[j] = p.gd.n

    emit_w1(0)
    emit_w3(0)
    emit_w2(0)
    emit_w1(1)
    last_th = -1
    for j, (th, ex) in enumerate(steps):
        woff = (j % 2) * 8 * DFF
        if th != last_th:
            last_th = th
            p.wait("vector", p.mm)
            p.vop("memset", acc_sb[:, :], 0.0)
        p.wait("tensor", p.gd, gd_w13[j])
        p.wait("tensor", p.gd, gd_w3[j])
        for kt in range(2):
            t0 = th * TCW + kt * W
            ta = kt * W
            p.wait("gpsimd", p.ve)
            p.dma(crep_f[:, :], g["comb_fm"][ex:ex + 1, t0:t0 + W]
                  .broadcast_to([128, W]), eng="gpsimd")
            crep_n = p.gd.n
            p.wait("vector", p.mm)
            for fb in range(8):
                q = fb % 2
                q2 = 2 + fb % 2
                guard(q); guard(q2)
                for ch in range(8):
                    p.mmul(pp[q][:, 0:W],
                           we1[:, woff + ch * DFF + fb * 128:woff + ch * DFF + fb * 128 + 128],
                           xf2_sb[:, ch * TLOC + t0:ch * TLOC + t0 + W],
                           ch == 0, ch == 7, inc_last=False)
                    p.mmul(pp[q2][:, 0:W],
                           we3[:, ch * DFF + fb * 128:ch * DFF + fb * 128 + 128],
                           xf2_sb[:, ch * TLOC + t0:ch * TLOC + t0 + W],
                           ch == 0, ch == 7)
                p.wait("scalar", p.mm)
                p.wait("scalar", p.ve)
                p.sop("activation", h1_bf[:, 0:W], pp[q][:, 0:W],
                      AF.Silu)
                ppg[q].append((p.sc, p.sc.n))
                p.wait("vector", p.sc)
                p.wait("vector", p.gd, crep_n)
                p.vop("tensor_mul", t32b[:, 0:W], crep_f[:, 0:W], pp[q2][:, 0:W],
                      inc=False)
                p.vop("tensor_mul", hp_bf[:, fb * W:(fb + 1) * W],
                      h1_bf[:, 0:W], t32b[:, 0:W])
                ppg[q2].append((p.ve, p.ve.n))
            if kt == 1:
                mm_stage1[j] = p.mm.n
                if j + 2 < len(steps):
                    emit_w1(j + 2)
                if j + 1 < len(steps):
                    emit_w3(j + 1)
            if kt == 0:
                p.wait("tensor", p.gd, gd_w2[j])
            for ob in range(8):
                q = 4 + ob % 2
                guard(q)
                p.wait("tensor", p.ve)
                for fb in range(8):
                    p.mmul(pp[q][:, 0:W],
                           we2[:, fb * D + ob * 128:fb * D + ob * 128 + 128],
                           hp_bf[:, fb * W:(fb + 1) * W], fb == 0, fb == 7)
                p.wait("vector", p.mm)
                p.vop("tensor_add", acc_sb[:, ob * TCW + ta:ob * TCW + ta + W],
                      acc_sb[:, ob * TCW + ta:ob * TCW + ta + W], pp[q][:, 0:W])
                ppg[q].append((p.ve, p.ve.n))
        mm_step[j] = p.mm.n
        if j + 1 < len(steps):
            emit_w2(j + 1)
        if ex != 8:
            continue
        for kt in range(2):
            t0 = th * TCW + kt * W
            ta = kt * W
            p.wait("sync", p.ve)
            for ch in range(8):
                p.dma(x_sb[:, ch * W:(ch + 1) * W],
                      g["x2_sp"][128 * ch:(ch + 1) * 128, t0:t0 + W])
            for ch in range(8):
                sl = slice(ch * W, (ch + 1) * W)
                p.dma(t32b[:, 0:W], g["x3_sp"][128 * ch:(ch + 1) * 128, t0:t0 + W])
                p.wait("vector", p.ld)
                p.vop("tensor_add", t32a[:, 0:W], x_sb[:, sl], t32b[:, 0:W], inc=False)
                p.vop("tensor_add", x2_sb[:, sl], t32a[:, 0:W],
                      acc_sb[:, ch * TCW + ta:ch * TCW + ta + W], inc=True)
                p.wait("sync", p.ve)
                p.emit("sync", "dma_start", inc=(p.st, 16),
                       out=g["out_fm"][128 * ch:(ch + 1) * 128, t0:t0 + W],
                       in_=x2_sb[:, ch * W:(ch + 1) * W])


def run(inputs, taps=(), phases="full"):
    in_maps = host_prepare(inputs)
    nc = build_nc(taps=taps, phases=phases)
    res = run_bass_kernel_spmd(nc, in_maps, core_ids=list(range(8)))
    return res


def kernel(**inputs):
    res = run(inputs, taps=(), phases="full")
    return host_finalize(res.results)
